# version 02
# baseline (speedup 1.0000x reference)
import contextlib
import numpy as np
import ml_dtypes
import concourse.bass as bass
import concourse.mybir as mybir
from concourse.bass_utils import run_bass_kernel_spmd

F32 = mybir.dt.float32
BF16 = mybir.dt.bfloat16
I32 = mybir.dt.int32
AF = mybir.ActivationFunctionType
ALU = mybir.AluOpType
BF = ml_dtypes.bfloat16

MAGIC = float(3 * 2.0 ** 22)
PIN = 384.5
PINI = 384.0
ANDMASK = 0xFFFF8000 - (1 << 32)


def mkcfg(D, T, n_cores):
    NH = D // 64
    c = dict(
        D=D, T=T, n_cores=n_cores, NH=NH, NPAIR=NH // 2,
        CONV=D + 128, NCB=D // 128 + 1, NZB=D // 128,
        DIP=2 * D + 128 + NH, NCH=T // 128, NT=T // 128,
        TH=T + 3,
    )
    c["XNS"] = ((c["TH"] + 15) // 16) * 16
    strips = []
    off = 0
    while off < c["TH"]:
        w = min(512, c["TH"] - off)
        strips.append((off, w))
        off += w
    c["XBC_STRIPS"] = strips
    zstrips = []
    off = 3
    while off < c["TH"]:
        w = min(512, c["TH"] - off)
        zstrips.append((off, w))
        off += w
    c["Z_STRIPS"] = zstrips
    c["OSW"] = min(256, D)
    c["NOS"] = D // c["OSW"]
    return c


class Rec:

    ENGINES = ("gpsimd", "sync", "scalar", "vector", "tensor")

    def __init__(self):
        self.ops = {e: [] for e in self.ENGINES}
        self.ctr = {s: 0 for s in ("te", "vv", "ss", "cc")}
        self.streams = {}
        self._vw = set()
        self._spacer_fn = None
        self.n_spacers = 0

    def _sem(self, name):
        if name not in self.ctr:
            self.ctr[name] = 0
        return name

    def emit(self, eng, fn, waits=(), inc=None, inc_amt=1, reads=None, writes=None):
        if eng == "vector":
            rd = {"*"} if reads is None else set(reads)
            wr = {"*"} if writes is None else set(writes)
            prev = self._vw
            if prev and rd and ("*" in prev or "*" in rd or (rd & prev)):
                assert self._spacer_fn is not None
                self.ops["vector"].append((self._spacer_fn, [], None))
                self.n_spacers += 1
            self._vw = wr
        w = []
        for s, v in waits:
            if v > 0:
                w.append((s, v))
        i = None
        if inc is not None:
            self._sem(inc)
            self.ctr[inc] += inc_amt
            i = (inc, inc_amt)
        self.ops[eng].append((fn, w, i))

    def dma(self, eng, fn, waits=(), stream="misc", K=2):
        st = self.streams.setdefault(stream, {"n": 0, "K": K})
        assert st["K"] == K, f"stream {stream} K mismatch"
        n = st["n"]
        slot = n % K
        sem = self._sem(f"d_{stream}_{slot}")
        prior = 16 * (n // K)
        self.emit(eng, fn, list(waits) + [(sem, prior)], inc=sem, inc_amt=16)
        st["n"] = n + 1
        return (sem, 16 * (n // K + 1))

    def fence(self, *streams):
        out = []
        names = streams if streams else list(self.streams.keys())
        for nm in names:
            st = self.streams.get(nm)
            if not st:
                continue
            K = st["K"]
            for slot in range(K):
                cnt = (st["n"] + K - 1 - slot) // K
                if cnt > 0:
                    out.append((f"d_{nm}_{slot}", 16 * cnt))
        return out

    def val(self, sem):
        return self.ctr[sem]

    def replay(self, eng_name, eng, sems):
        last = {}
        for fn, waits, inc in self.ops[eng_name]:
            for s, v in waits:
                if last.get(s, 0) < v:
                    eng.wait_ge(sems[s], v)
                    last[s] = v
            inst = fn(eng)
            if inc is not None:
                s, amt = inc
                inst.then_inc(sems[s], amt)


def _finish(nc, es, R):
    sems = {name: es.enter_context(nc.semaphore(name)) for name in R.ctr}
    block = es.enter_context(nc.Block())

    @block.gpsimd
    def _(g):
        R.replay("gpsimd", g, sems)

    @block.sync
    def _(sy):
        R.replay("sync", sy, sems)

    @block.scalar
    def _(s):
        R.replay("scalar", s, sems)

    @block.vector
    def _(v):
        R.replay("vector", v, sems)

    @block.tensor
    def _(t):
        R.replay("tensor", t, sems)

    es.close()
    return nc


def build_nc(c, detect_races=False, upto=None, dbg=False):
    D, T, NH = c["D"], c["T"], c["NH"]
    NPAIR, CONV, NCB, NZB = c["NPAIR"], c["CONV"], c["NCB"], c["NZB"]
    NCH, NT, TH, XNS = c["NCH"], c["NT"], c["TH"], c["XNS"]
    n_cores = c["n_cores"]
    NCT = D // 128
    NHH = NH // 2
    OSW = c["OSW"]
    STKB = 32

    nc = bass.Bass("TRN2", target_bir_lowering=False, debug=False,
                   detect_race_conditions=detect_races)

    def din(name, shape, dt=F32):
        return nc.dram_tensor(name, shape, dt, kind="ExternalInput").ap()

    u_loc = din("u_loc", [T, D])
    u_halo = din("u_halo", [128, D])
    w_inT = din("w_inT", [D, c["DIP"]])
    w_outT = din("w_outT", [D, D])
    w_ffnT = din("w_ffnT", [D, D])
    conv_w_l = din("conv_w_l", [128, NCB * 4])
    conv_b_l = din("conv_b_l", [128, NCB])
    dt_bias_c = din("dt_bias_c", [NH, 1])
    a_log_c = din("a_log_c", [NH, 1])
    d_rep = din("d_rep", [1, D])
    sel_in = din("sel_in", [64, n_cores])
    caus_in = din("caus_in", [128, 128])
    causneg_in = din("causneg_in", [128, 128])
    eye_in = din("eye_in", [128, 128])
    ones_in = din("ones_in", [1, 128])
    onecol_in = din("onecol_in", [128, 1])
    zer32_in = din("zer32_in", [NH, 128])
    zpad_in = din("zpad_in", [16, D], BF16)
    cvals_in = din("cvals_in", [128, 8])

    out_loc = nc.dram_tensor("out_loc", [T, D], F32, kind="ExternalOutput").ap()

    kd = dict(kind="ExternalOutput") if dbg else {}
    xn_hbm = nc.dram_tensor("xn_hbm", [XNS, D], BF16, **kd).ap()
    xq_hbm = nc.dram_tensor("xq_hbm", [T, D], BF16, **kd).ap()
    xconv_hbm = nc.dram_tensor("xconv_hbm", [CONV, T], BF16, **kd).ap()
    siluz_hbm = nc.dram_tensor("siluz_hbm", [D, T], BF16, **kd).ap()
    cum_hbm = nc.dram_tensor("cum_hbm", [NH, T], F32, **kd).ap()
    wq_hbm = nc.dram_tensor("wq_hbm", [D, D], BF16, **kd).ap()
    ws_hbm = nc.dram_tensor("ws_hbm", [1, 3], F32).ap()
    cc_in = nc.dram_tensor("cc_in", [64, NH * 64], F32).ap()
    cc_out = nc.dram_tensor("cc_out", [64 * n_cores, NH * 64], F32,
                            addr_space="Shared" if n_cores > 4 else "Local").ap()

    es = contextlib.ExitStack()
    _auto = []

    def sbuf(name, shape, dt=F32):
        h = es.enter_context(nc.sbuf_tensor(name, shape, dt))
        _auto.append((h, shape, dt))
        return h

    SBUF_CAP = 229344

    def tsize(shape, dt):
        n = 1
        for s in shape[1:]:
            n *= s
        return ((n * mybir.dt.size(dt) + 63) // 64) * 64

    specA, specC = {}, {}

    def chain(spec, items):
        cur = 0
        for nm, shp, dt in items:
            spec[nm] = (shp, dt, cur)
            cur += tsize(shp, dt)
        return cur

    wA0 = chain(specA, [("ub0", [128, D], F32), ("ub1", [128, D], F32),
                        ("xnf", [128, D], F32), ("xnb", [128, D], BF16),
                        ("qt", [128, D], F32), ("xqb", [128, D], BF16),
                        ("wf0", [128, D], F32), ("wf1", [128, D], F32)])
    wA1 = chain(specA, [("sc0", [32, T], F32), ("sc1", [32, T], F32),
                        ("sc2", [32, T], F32), ("sc3", [32, T], F32),
                        ("stk", [3 * STKB, T], F32)])
    wA2 = chain(specA, [("XT0", [128, CONV], BF16), ("XT1", [128, CONV], BF16),
                        ("bsall", [128, NH * 64], BF16), ("BC2a", [64, 256], BF16),
                        ("BC2b", [64, 256], BF16), ("gmask", [128, 128], BF16),
                        ("szc", [128, D], BF16), ("y0", [128, D], BF16),
                        ("slots0", [64, NH * 64], F32), ("slots1", [64, NH * 64], F32)])
    specA["xq_ch"] = ([128, NCT * T], BF16, 0)
    widthA = max(wA0, wA1, wA2, tsize([128, NCT * T], BF16))

    widthB = max(tsize([128, NCT * XNS], BF16), tsize([128, NCT * T], BF16))

    wC0 = chain(specC, [("wstage0", [128, NCT * 128], BF16), ("wstage1", [128, NCT * 128], BF16),
                        ("stage", [128, TH], BF16), ("convacc", [128, T], F32),
                        ("xcb", [128, T], BF16), ("dt_toks", [NH, T], F32)])
    wC1 = chain(specC, [("cumrep0", [128, NHH * 128], F32), ("cumrep1", [128, NHH * 128], F32),
                        ("dall", [128, NH * 128], BF16), ("mall", [128, NH * 128], BF16),
                        ("t1", [128, D], F32), ("xd", [128, D], BF16), ("t2", [128, D], BF16)])
    wC2 = chain(specC, [("wos0", [128, NCT * OSW], BF16), ("wos1", [128, NCT * OSW], BF16),
                        ("wqs0", [128, NCT * OSW], BF16), ("wqs1", [128, NCT * OSW], BF16),
                        ("uts0", [128, OSW], F32), ("uts1", [128, OSW], F32),
                        ("uts2", [128, OSW], F32), ("uts3", [128, OSW], F32),
                        ("outs0", [128, OSW], F32), ("outs1", [128, OSW], F32),
                        ("outs2", [128, OSW], F32), ("outs3", [128, OSW], F32)])
    widthC = max(wC0, wC1, wC2)

    ARENA = widthA + widthB + widthC
    A0 = SBUF_CAP - ARENA - 1024
    offB, offC = widthA, widthA + widthB

    def at(spec, base, nm):
        shp, dt, off = spec[nm]
        return nc.alloc_sbuf_tensor_at(nm, shp, dt, offset=A0 + base + off).ap()

    caus = sbuf("caus", [128, 128])
    eye = sbuf("eye", [128, 128])
    ones1 = sbuf("ones1", [1, 128])
    onecol = sbuf("onecol", [128, 1])
    zer32 = sbuf("zer32", [NH, 128])
    cwsb = sbuf("cwsb", [128, NCB * 4])
    cbsb = sbuf("cbsb", [128, NCB])
    dtb = sbuf("dtb", [NH, 1])
    alog = sbuf("alog", [NH, 1])
    drep_bf = sbuf("drep_bf", [128, D], BF16)
    sel = sbuf("sel", [64, n_cores])
    cvals = sbuf("cvals", [128, 8])
    mv = sbuf("mv", [128, 2])
    lnt = sbuf("lnt", [128, 4])
    amx = sbuf("amx", [128, 4])
    wsacc = sbuf("wsacc", [128, 1])
    wscol2 = sbuf("wscol2", [128, 2])
    ws11 = sbuf("ws11", [1, 4])
    wsrow = sbuf("wsrow", [1, 128])
    TTa = sbuf("TTa", [128, NCH * NH])
    ecT = sbuf("ecT", [128, NCH * NH])
    scolT = sbuf("scolT", [128, NCH * NH], BF16)
    eQa = sbuf("eQa", [64, NCH * NH])
    cqr = sbuf("cqr", [1, NH])
    s_run = sbuf("s_run", [64, NH * 64])
    s_bf = sbuf("s_bf", [64, NH * 64], BF16)
    spc = sbuf("spc", [128, 1])

    stack_top = max(nc.lookup_mloc(h).addr + tsize(shp, dt) for h, shp, dt in _auto)
    assert stack_top <= A0, f"left stack {stack_top} overlaps arena base {A0}"

    ub = [at(specA, 0, "ub0"), at(specA, 0, "ub1")]
    xnf = at(specA, 0, "xnf")
    xnb = at(specA, 0, "xnb")
    qt = at(specA, 0, "qt")
    xqb = at(specA, 0, "xqb")
    wf = [at(specA, 0, "wf0"), at(specA, 0, "wf1")]
    wqt = qt
    wqb = xqb
    wabs = xnf
    sc_ = [at(specA, 0, f"sc{i}") for i in range(4)]
    stk = at(specA, 0, "stk")
    XT = [at(specA, 0, "XT0"), at(specA, 0, "XT1")]
    bsall = at(specA, 0, "bsall")
    BC2 = [at(specA, 0, "BC2a"), at(specA, 0, "BC2b")]
    gmask = at(specA, 0, "gmask")
    szc = at(specA, 0, "szc")
    y0 = at(specA, 0, "y0")
    sslot = [at(specA, 0, "slots0"), at(specA, 0, "slots1")]
    xq_ch = at(specA, 0, "xq_ch")
    xn_ch = nc.alloc_sbuf_tensor_at("xn_ch", [128, NCT * XNS], BF16, offset=A0 + offB).ap()
    ych = nc.alloc_sbuf_tensor_at("ych", [128, NCT * T], BF16, offset=A0 + offB).ap()
    wstage = [at(specC, offC, "wstage0"), at(specC, offC, "wstage1")]
    stage = at(specC, offC, "stage")
    convacc = at(specC, offC, "convacc")
    xcb = at(specC, offC, "xcb")
    dt_toks = at(specC, offC, "dt_toks")
    cumrep = [at(specC, offC, "cumrep0"), at(specC, offC, "cumrep1")]
    dall = at(specC, offC, "dall")
    mall = at(specC, offC, "mall")
    t1 = at(specC, offC, "t1")
    xd = at(specC, offC, "xd")
    t2 = at(specC, offC, "t2")
    wos = [at(specC, offC, "wos0"), at(specC, offC, "wos1")]
    wqs = [at(specC, offC, "wqs0"), at(specC, offC, "wqs1")]
    uts = [at(specC, offC, f"uts{i}") for i in range(4)]
    outs = [at(specC, offC, f"outs{i}") for i in range(4)]

    PA = es.enter_context(nc.psum_tensor("PA", [128, 2048], F32))
    PB = es.enter_context(nc.psum_tensor("PB", [128, 2048], F32))

    R = Rec()
    R._spacer_fn = lambda v: v.tensor_copy(spc[:], onecol[:])

    def cut(name):
        return upto is not None and upto == name

    def finale():
        R.emit("vector", lambda v: v.engine_nop(),
               waits=R.fence() + [("te", R.val("te")), ("ss", R.val("ss"))],
               reads=(), writes=())
        return _finish(nc, es, R)

    for dst, src in ((caus, caus_in), (eye, eye_in), (ones1, ones_in),
                     (onecol, onecol_in), (zer32, zer32_in), (cwsb, conv_w_l),
                     (cbsb, conv_b_l), (dtb, dt_bias_c), (alog, a_log_c),
                     (sel, sel_in), (cvals, cvals_in)):
        R.dma("gpsimd", lambda g, dst=dst, src=src: g.dma_start(out=dst[:], in_=src), stream="cst")
    R.dma("gpsimd", lambda g: g.dma_start(out=drep_bf[:], in_=d_rep[0, :].partition_broadcast(128)),
          stream="cst")
    if XNS > TH:
        R.dma("gpsimd", lambda g: g.dma_start(out=xn_hbm[TH:XNS, :], in_=zpad_in[0:XNS - TH, :]),
              stream="cst")
    F_cst = R.fence("cst")
    for e in Rec.ENGINES:
        R.emit(e, lambda eng: None, waits=F_cst)

    p0_vv_end = {}

    def p0_tile(i, halo):
        buf = ub[(i + 1) % 2 if halo else i % 2]
        src = u_halo if halo else u_loc[i * 128:(i + 1) * 128, :]
        key = -1 if halo else i
        wu = R.dma("gpsimd", lambda g, buf=buf, src=src: g.dma_start(out=buf[:], in_=src),
                   waits=[("vv", p0_vv_end.get(key - 2, 0))], stream="u0")
        w0 = [wu, ("ss", R.val("ss"))] + R.fence("wxn", "wxq")
        bnm = f"ub{(i + 1) % 2 if halo else i % 2}"
        R.emit("vector", lambda v, buf=buf: v.reduce_sum(mv[:, 0:1], buf[:], axis=mybir.AxisListType.X),
               waits=w0, reads=(bnm,), writes=("mv",))
        R.emit("vector", lambda v: v.tensor_scalar(mv[:, 0:1], mv[:, 0:1], 1.0 / D, None, op0=ALU.mult),
               reads=("mv",), writes=("mv",))
        R.emit("vector", lambda v, buf=buf: v.tensor_scalar(xnf[:], buf[:], mv[:, 0:1], None,
                                                            op0=ALU.subtract), inc="vv",
               reads=(bnm, "mv"), writes=("xnf",))
        vv_um = R.val("vv")
        R.emit("scalar", lambda s: s.activation(qt[:], xnf[:], AF.Square), waits=[("vv", vv_um)], inc="ss")
        ss_sq = R.val("ss")
        R.emit("vector", lambda v: v.reduce_sum(mv[:, 1:2], qt[:], axis=mybir.AxisListType.X),
               waits=[("ss", ss_sq)], reads=("qt",), writes=("mv",))
        R.emit("vector", lambda v: v.tensor_scalar(mv[:, 1:2], mv[:, 1:2], 1.0 / D, None, op0=ALU.mult),
               inc="vv", reads=("mv",), writes=("mv",))
        vv_mv = R.val("vv")
        R.emit("scalar", lambda s: s.activation(lnt[:, 0:1], mv[:, 1:2], AF.Ln, bias=cvals[:, 0:1],
                                                scale=1.0),
               waits=[("vv", vv_mv)])
        R.emit("scalar", lambda s: s.activation(lnt[:, 1:2], lnt[:, 0:1], AF.Exp, scale=cvals[:, 1:2]),
               inc="ss")
        ss_r = R.val("ss")
        R.emit("vector", lambda v: v.tensor_scalar(xnf[:], xnf[:], lnt[:, 1:2], None, op0=ALU.mult),
               waits=[("ss", ss_r)], reads=("xnf", "lnt"), writes=("xnf",))
        R.emit("vector", lambda v: v.tensor_copy(xnb[:], xnf[:]), inc="vv",
               reads=("xnf",), writes=("xnb",))
        vv_xnb = R.val("vv")
        R.dma("scalar", lambda s, i=i, halo=halo: s.dma_start(
            out=xn_hbm[0:3, :] if halo else xn_hbm[3 + i * 128: 3 + (i + 1) * 128, :],
            in_=xnb[0:3, :] if halo else xnb[:]),
            waits=[("vv", vv_xnb)], stream="wxn")
        if not halo:
            R.emit("vector", lambda v: v.reduce_max(amx[:, 0:1], xnf[:], axis=mybir.AxisListType.X,
                                                    apply_absolute_value=True),
                   reads=("xnf",), writes=("amx",))
            R.emit("vector", lambda v: v.tensor_scalar_max(amx[:, 1:2], amx[:, 0:1], 1e-8), inc="vv",
                   reads=("amx",), writes=("amx",))
            vv_am = R.val("vv")
            R.emit("scalar", lambda s: s.activation(lnt[:, 2:3], amx[:, 1:2], AF.Ln),
                   waits=[("vv", vv_am)])
            R.emit("scalar", lambda s: s.activation(amx[:, 2:3], lnt[:, 2:3], AF.Exp, scale=cvals[:, 2:3]),
                   inc="ss")
            ss_am = R.val("ss")
            R.emit("vector", lambda v: v.tensor_scalar(amx[:, 2:3], amx[:, 2:3], 127.0, None, op0=ALU.mult),
                   waits=[("ss", ss_am)], reads=("amx",), writes=("amx",))
            R.emit("vector", lambda v: v.tensor_scalar(amx[:, 3:4], amx[:, 1:2], 1.0 / 127.0, None,
                                                       op0=ALU.mult), reads=("amx",), writes=("amx",))
            R.emit("vector", lambda v: v.tensor_scalar(qt[:], xnf[:], amx[:, 2:3], MAGIC,
                                                       op0=ALU.mult, op1=ALU.add),
                   reads=("xnf", "amx"), writes=("qt",))
            R.emit("vector", lambda v: v.tensor_scalar(qt[:], qt[:], MAGIC, PIN,
                                                       op0=ALU.subtract, op1=ALU.add),
                   reads=("qt",), writes=("qt",))
            R.emit("vector", lambda v: v.tensor_single_scalar(qt[:].bitcast(I32), qt[:].bitcast(I32),
                                                              ANDMASK, op=ALU.bitwise_and),
                   reads=("qt",), writes=("qt",))
            R.emit("vector", lambda v: v.tensor_scalar(xqb[:], qt[:], PINI, amx[:, 3:4],
                                                       op0=ALU.subtract, op1=ALU.mult), inc="vv",
                   reads=("qt", "amx"), writes=("xqb",))
            vv_xq = R.val("vv")
            R.dma("scalar", lambda s, i=i: s.dma_start(out=xq_hbm[i * 128:(i + 1) * 128, :], in_=xqb[:]),
                  waits=[("vv", vv_xq)], stream="wxq")
        p0_vv_end[key] = R.val("vv")

    p0_tile(0, halo=True)
    for i in range(NT):
        p0_tile(i, halo=False)
    F_p0w = R.fence("wxn", "wxq")

    if cut("p0"):
        return finale()

    for cb in range(NCT):
        R.dma("sync", lambda sy, cb=cb: sy.dma_start_transpose(
            xn_ch[:, cb * XNS:(cb + 1) * XNS],
            xn_hbm[0:XNS, cb * 128:(cb + 1) * 128]),
            waits=F_p0w + F_cst, stream="lxn")
    F_xn = R.fence("lxn")

    if cut("xn"):
        return finale()

    def pa_strip(si, width):
        return PA[:, (si % 4) * 512:(si % 4) * 512 + width]

    p1_te_end = {}
    p1_wr_end = {}
    p1_ss_end = {}

    def emit_inproj(kind, key, bi, wcol0, wwidth, outparts):
        buf = wstage[key % 2]
        wws = R.dma("gpsimd", lambda g, buf=buf, wcol0=wcol0, wwidth=wwidth: g.dma_start(
            out=buf[:, 0:NCT * wwidth].rearrange("p (cc w) -> p cc w", w=wwidth),
            in_=w_inT[:, wcol0:wcol0 + wwidth].rearrange("(cc p) w -> p cc w", p=128)),
            waits=[("te", p1_te_end.get(key - 2, 0))], stream="wst")
        strips = c["XBC_STRIPS"] if kind == "xbc" else c["Z_STRIPS"]
        vv_before = R.val("vv")
        ss_before = R.val("ss")
        strip_evac = {}
        for si, (off, width) in enumerate(strips):
            pv = pa_strip(si, width)
            for ci in range(NCT):
                wl = []
                if ci == 0:
                    if si == 0:
                        wl = [wws, ("vv", vv_before), ("ss", ss_before)] + F_xn
                    elif si >= 4:
                        wl = [(("ss" if kind == "z" else "vv"), strip_evac[si - 4])]
                R.emit("tensor", lambda t, pv=pv, buf=buf, ci=ci, off=off, width=width,
                       wwidth=wwidth, outparts=outparts, first=(ci == 0), last=(ci == NCT - 1):
                       t.matmul(pv[0:outparts, :], buf[:, ci * wwidth:(ci + 1) * wwidth],
                                xn_ch[:, ci * XNS + off: ci * XNS + off + width],
                                start=first, stop=last),
                       waits=wl, inc="te" if ci == NCT - 1 else None)
            te_t = R.val("te")
            if kind == "xbc":
                R.emit("vector", lambda v, pv=pv, off=off, width=width:
                       v.tensor_copy(stage[:, off:off + width], pv),
                       waits=[("te", te_t)], inc="vv", reads=("PA",), writes=("stage",))
                strip_evac[si] = R.val("vv")
            elif kind == "z":
                R.emit("scalar", lambda s, pv=pv, off=off, width=width:
                       s.activation(xcb[:, off - 3:off - 3 + width], pv, AF.Silu),
                       waits=[("te", te_t), p1_wr_end.get(key - 1, ("ss", 0))], inc="ss")
                strip_evac[si] = R.val("ss")
            else:
                R.emit("vector", lambda v, pv=pv, off=off, width=width, outparts=outparts:
                       v.tensor_copy(dt_toks[:, off - 3:off - 3 + width], pv[0:outparts, :]),
                       waits=[("te", te_t)], inc="vv", reads=("PA",), writes=("dt_toks",))
        p1_te_end[key] = R.val("te")
        if kind == "xbc":
            R.emit("vector", lambda v, bi=bi:
                   v.tensor_scalar(convacc[:], stage[:, 0:T], cwsb[:, bi * 4:bi * 4 + 1], None,
                                   op0=ALU.mult),
                   waits=[("ss", p1_ss_end.get(key - 1, 0))],
                   reads=("stage", "cwsb"), writes=("convacc",))
            R.emit("vector", lambda v, bi=bi:
                   v.tensor_scalar(t1[:, 0:T], stage[:, 1:1 + T], cwsb[:, bi * 4 + 1:bi * 4 + 2], None,
                                   op0=ALU.mult),
                   reads=("stage", "cwsb"), writes=("cvt",))
            R.emit("vector", lambda v, bi=bi:
                   v.scalar_tensor_tensor(convacc[:], stage[:, 2:2 + T],
                                          cwsb[:, bi * 4 + 2:bi * 4 + 3],
                                          convacc[:], op0=ALU.mult, op1=ALU.add),
                   reads=("stage", "cwsb", "convacc"), writes=("convacc",))
            R.emit("vector", lambda v, bi=bi:
                   v.scalar_tensor_tensor(t1[:, 0:T], stage[:, 3:3 + T],
                                          cwsb[:, bi * 4 + 3:bi * 4 + 4],
                                          t1[:, 0:T], op0=ALU.mult, op1=ALU.add),
                   reads=("stage", "cwsb", "cvt"), writes=("cvt",))
            R.emit("vector", lambda v: v.tensor_add(convacc[:], convacc[:], t1[:, 0:T]), inc="vv",
                   reads=("convacc", "cvt"), writes=("convacc",))
            vv_conv = R.val("vv")
            R.emit("scalar", lambda s, bi=bi:
                   s.activation(xcb[:], convacc[:], AF.Silu, bias=cbsb[:, bi:bi + 1], scale=1.0),
                   waits=[("vv", vv_conv), p1_wr_end.get(key - 1, ("ss", 0))], inc="ss")
            p1_ss_end[key] = R.val("ss")
            p1_wr_end[key] = R.dma("scalar", lambda s, bi=bi: s.dma_start(
                out=xconv_hbm[bi * 128:(bi + 1) * 128, :], in_=xcb[:]),
                waits=[("ss", p1_ss_end[key])], stream="wcv")
        elif kind == "z":
            p1_ss_end[key] = R.val("ss")
            p1_wr_end[key] = R.dma("scalar", lambda s, bi=bi: s.dma_start(
                out=siluz_hbm[bi * 128:(bi + 1) * 128, :], in_=xcb[:]),
                waits=[("ss", p1_ss_end[key])], stream="wz")

    emit_inproj("dt", 0, 0, 2 * D + 128, NH, NH)
    vv_dt = R.val("vv")

    spm = sc_[0][0:NH, :]
    spa = sc_[1][0:NH, :]
    spe = sc_[2][0:NH, :]
    spl = sc_[3][0:NH, :]
    R.emit("vector", lambda v: v.tensor_scalar(spm, dt_toks[:], dtb[:], None, op0=ALU.add),
           waits=[("vv", vv_dt)] + F_cst + F_p0w, inc="vv",
           reads=("dt_toks", "dtb"), writes=("sc0",))
    vv_t = R.val("vv")
    R.emit("scalar", lambda s: s.activation(spa, spm, AF.Abs), waits=[("vv", vv_t)])
    R.emit("scalar", lambda s: s.activation(spe, spa, AF.Exp, scale=cvals[0:NH, 2:3]))
    R.emit("scalar", lambda s: s.activation(spl, spe, AF.Ln, bias=cvals[0:NH, 3:4]),
           waits=R.fence("wxn"), inc="ss")
    ss_t = R.val("ss")
    mx = sc_[2][0:NH, :]
    dtv = sc_[1][0:NH, :]
    logda = sc_[0][0:NH, :]
    logdt = sc_[3][0:NH, :]
    R.emit("vector", lambda v: v.tensor_scalar_max(mx, spm, 0.0), waits=[("ss", ss_t)],
           reads=("sc0",), writes=("sc2",))
    R.emit("vector", lambda v: v.tensor_add(dtv, mx, spl),
           reads=("sc2", "sc3"), writes=("sc1",))
    R.emit("scalar", lambda s: s.activation(lnt[0:NH, 3:4], alog[:], AF.Exp), inc="ss")
    ss_t = R.val("ss")
    R.emit("vector", lambda v: v.tensor_scalar(logda, dtv, lnt[0:NH, 3:4], -1.0,
                                               op0=ALU.mult, op1=ALU.mult),
           waits=[("ss", ss_t)], inc="vv", reads=("sc1", "lnt"), writes=("sc0",))
    vv_t = R.val("vv")
    R.emit("scalar", lambda s: s.activation(logdt, dtv, AF.Ln), waits=[("vv", vv_t)], inc="ss")
    ss_logdt = R.val("ss")
    if NH < STKB:
        R.emit("vector", lambda v: v.memset(stk[:], 0.0), reads=(), writes=("stk",))
    for ch in range(NCH):
        sl = slice(ch * 128, (ch + 1) * 128)
        R.emit("vector", lambda v, sl=sl: v.tensor_tensor_scan(
            stk[0:NH, sl], logda[0:NH, sl], zer32[:, 0:128], 0.0, op0=ALU.add, op1=ALU.add),
            waits=([("ss", ss_logdt)] + R.fence("wxq")) if ch == 0 else (),
            reads=("sc0", "zer32"), writes=("stk",))
        R.emit("vector", lambda v, sl=sl: v.scalar_tensor_tensor(
            stk[STKB:STKB + NH, sl], stk[0:NH, sl], -1.0, logdt[0:NH, sl],
            op0=ALU.mult, op1=ALU.add),
            reads=("stk", "sc3"), writes=("stk",))
        R.emit("vector", lambda v, sl=sl, ch=ch: v.tensor_scalar(
            stk[2 * STKB:2 * STKB + NH, sl], stk[STKB:STKB + NH, sl],
            stk[0:NH, ch * 128 + 127:ch * 128 + 128], None, op0=ALU.add),
            reads=("stk",), writes=("stk",),
            inc="vv" if ch == NCH - 1 else None)
    vv_stk = R.val("vv")
    w_cum = R.dma("scalar", lambda s: s.dma_start(out=cum_hbm[:, :], in_=stk[0:NH, :]),
                  waits=[("vv", vv_stk)], stream="wcm", K=1)
    for ch in range(NCH):
        R.emit("tensor", lambda t, ch=ch: t.transpose(
            PB[:, 1024:1024 + 3 * STKB], stk[:, ch * 128:(ch + 1) * 128], eye[0:3 * STKB, 0:3 * STKB]),
            waits=[("vv", vv_stk if ch == 0 else R.val("vv")), ("ss", R.val("ss"))], inc="te")
        te_t = R.val("te")
        R.emit("vector", lambda v, ch=ch: v.tensor_copy(
            TTa[:, ch * NH:(ch + 1) * NH], PB[:, 1024 + STKB:1024 + STKB + NH]),
            waits=[("te", te_t)], inc="vv", reads=("PB",), writes=("TTa",))
        vv_tt = R.val("vv")
        R.emit("scalar", lambda s, ch=ch: s.activation(
            ecT[:, ch * NH:(ch + 1) * NH], PB[:, 1024:1024 + NH], AF.Exp),
            waits=[("te", te_t)])
        R.emit("scalar", lambda s, ch=ch: s.activation(
            scolT[:, ch * NH:(ch + 1) * NH],
            PB[:, 1024 + 2 * STKB:1024 + 2 * STKB + NH], AF.Exp),
            inc="ss")
        ss_exp = R.val("ss")
        R.emit("tensor", lambda t, ch=ch: t.transpose(
            PB[0:1, 1200:1200 + NH], stk[0:NH, ch * 128 + 127:ch * 128 + 128], eye[0:NH, 0:NH]),
            waits=[("vv", vv_tt), ("ss", ss_exp)], inc="te")
        te_t = R.val("te")
        R.emit("vector", lambda v: v.tensor_copy(cqr[:], PB[0:1, 1200:1200 + NH]),
               waits=[("te", te_t)], inc="vv", reads=("PB",), writes=("cqr",))
        vv_t = R.val("vv")
        R.emit("tensor", lambda t: t.matmul(PB[0:64, 1216:1216 + NH], ones1[:, 0:64], cqr[:],
                                            start=True, stop=True),
               waits=[("vv", vv_t)], inc="te")
        te_t = R.val("te")
        R.emit("scalar", lambda s, ch=ch: s.activation(
            eQa[:, ch * NH:(ch + 1) * NH], PB[0:64, 1216:1216 + NH], AF.Exp),
            waits=[("te", te_t)], inc="ss")
    ss_pre = R.val("ss")

    if cut("pre"):
        return finale()

    NWT = D // 128
    ws_ss_end = {}
    for j in range(NWT):
        b = wf[j % 2]
        wwf = R.dma("gpsimd", lambda g, b=b, j=j: g.dma_start(out=b[:], in_=w_ffnT[j * 128:(j + 1) * 128, :]),
                    waits=[("ss", ws_ss_end.get(j - 2, 0))], stream="wf")
        R.emit("scalar", lambda s, b=b: s.activation(wabs[:], b[:], AF.Abs),
               waits=[wwf, ("vv", R.val("vv"))] + (F_p0w if j == 0 else []), inc="ss")
        ss_t = R.val("ss")
        ws_ss_end[j] = ss_t
        if j == 0:
            R.emit("vector", lambda v: v.reduce_sum(wsacc[:], wabs[:], axis=mybir.AxisListType.X),
                   waits=[("ss", ss_t)], inc="vv", reads=("xnf",), writes=("wsacc",))
        else:
            R.emit("vector", lambda v: v.reduce_sum(lnt[:, 3:4], wabs[:], axis=mybir.AxisListType.X),
                   waits=[("ss", ss_t)], reads=("xnf",), writes=("lnt",))
            R.emit("vector", lambda v: v.tensor_add(wsacc[:], wsacc[:], lnt[:, 3:4]), inc="vv",
                   reads=("wsacc", "lnt"), writes=("wsacc",))
    vv_ws = R.val("vv")
    R.emit("tensor", lambda t: t.transpose(PB[0:1, 0:128], wsacc[:], eye[:, 0:128]),
           waits=[("vv", vv_ws)] + F_cst, inc="te")
    te_ws = R.val("te")
    R.emit("vector", lambda v: v.tensor_copy(wsrow[:], PB[0:1, 0:128]),
           waits=[("te", te_ws)], reads=("PB",), writes=("wsrow",))
    R.emit("vector", lambda v: v.reduce_sum(ws11[:, 0:1], wsrow[:], axis=mybir.AxisListType.X),
           reads=("wsrow",), writes=("ws11",))
    R.emit("vector", lambda v: v.tensor_scalar(ws11[:, 1:2], ws11[:, 0:1], 1.0 / (D * D), None,
                                               op0=ALU.mult), reads=("ws11",), writes=("ws11",))
    R.emit("vector", lambda v: v.tensor_scalar_max(ws11[:, 1:2], ws11[:, 1:2], 1e-6), inc="vv",
           reads=("ws11",), writes=("ws11",))
    vv_wsm = R.val("vv")
    R.emit("scalar", lambda s: s.activation(ws11[:, 3:4], ws11[:, 1:2], AF.Ln),
           waits=[("vv", vv_wsm)])
    R.emit("scalar", lambda s: s.activation(ws11[:, 2:3], ws11[:, 3:4], AF.Exp, scale=cvals[0:1, 2:3]),
           inc="ss")
    ss_wsc = R.val("ss")
    w_wsh = R.dma("scalar", lambda s: s.dma_start(out=ws_hbm[:, 0:2], in_=ws11[:, 1:3]),
                  waits=[("ss", ss_wsc)], stream="wsw", K=1)
    w_wsr = R.dma("gpsimd", lambda g: g.dma_start(out=wscol2[:], in_=ws_hbm[0, 0:2].partition_broadcast(128)),
                  waits=[w_wsh], stream="wsr", K=1)

    wq_vv_end = {}
    for j in range(NWT):
        b = wf[j % 2]
        wwf = R.dma("gpsimd", lambda g, b=b, j=j: g.dma_start(out=b[:], in_=w_ffnT[j * 128:(j + 1) * 128, :]),
                    waits=[("vv", wq_vv_end.get(j - 2, 0))] +
                          ([("ss", ws_ss_end[NWT - 1])] if j < 2 else []),
                    stream="wf")
        R.emit("vector", lambda v, b=b: v.tensor_scalar(wqt[:], b[:], wscol2[:, 1:2], MAGIC,
                                                        op0=ALU.mult, op1=ALU.add),
               waits=[wwf, w_wsr, ("te", R.val("te"))] +
                     ((R.fence("wxq") + [w_cum]) if j == 0 else []),
               reads=(f"wf{j % 2}", "wscol2"), writes=("qt",))
        R.emit("vector", lambda v: v.tensor_scalar(wqt[:], wqt[:], MAGIC, PIN,
                                                   op0=ALU.subtract, op1=ALU.add),
               reads=("qt",), writes=("qt",))
        R.emit("vector", lambda v: v.tensor_single_scalar(wqt[:].bitcast(I32), wqt[:].bitcast(I32),
                                                          ANDMASK, op=ALU.bitwise_and),
               reads=("qt",), writes=("qt",))
        R.emit("vector", lambda v: v.tensor_scalar(wqt[:], wqt[:], PINI, None, op0=ALU.subtract),
               reads=("qt",), writes=("qt",))
        R.emit("vector", lambda v: v.tensor_scalar(wqt[:], wqt[:], -1.0, 1.0, op0=ALU.max, op1=ALU.min),
               reads=("qt",), writes=("qt",))
        R.emit("vector", lambda v: v.tensor_scalar(wqb[:], wqt[:], wscol2[:, 0:1], None, op0=ALU.mult),
               inc="vv", reads=("qt", "wscol2"), writes=("xqb",))
        vv_wq = R.val("vv")
        wq_vv_end[j] = vv_wq
        R.dma("scalar", lambda s, j=j: s.dma_start(out=wq_hbm[j * 128:(j + 1) * 128, :], in_=wqb[:]),
              waits=[("vv", vv_wq)], stream="wwq")
    F_wq = R.fence("wwq")

    if cut("w"):
        return finale()

    for bi in range(NCB):
        emit_inproj("xbc", 1 + bi, bi, D + bi * 128, 128, 128)
    F_cv = R.fence("wcv")

    if cut("p1x"):
        return finale()

    R.emit("vector", lambda v: v.memset(s_run[:], 0.0), inc="vv", reads=(), writes=("s_run",))

    def load_xt(ch):
        xb = XT[ch % 2]
        return R.dma("sync", lambda sy, xb=xb, ch=ch: sy.dma_start_transpose(
            xb[:], xconv_hbm[0:CONV, ch * 128:(ch + 1) * 128]),
            waits=R.fence("wcv") + [("te", R.val("te")), ("vv", R.val("vv"))], stream="lxt")

    def emit_states(ch, w_xt):
        xb = XT[ch % 2]
        R.emit("vector", lambda v, xb=xb, ch=ch: v.tensor_tensor(
            bsall[:].rearrange("p (h n) -> p h n", h=NH),
            xb[:, D:D + 64].rearrange("p (o n) -> p o n", o=1).to_broadcast([128, NH, 64]),
            scolT[:, ch * NH:(ch + 1) * NH].rearrange("p (h o) -> p h o", o=1).to_broadcast([128, NH, 64]),
            op=ALU.mult),
            waits=[w_xt, ("ss", ss_pre), ("te", R.val("te"))], inc="vv",
            reads=(f"XT{ch % 2}", "scolT"), writes=("bsall",))
        vv_bs = R.val("vv")
        for h in range(NH):
            first = (h % 8 == 0)
            last = (h % 8 == 7) or h == NH - 1
            R.emit("tensor", lambda t, h=h, xb=xb, first=first, last=last:
                   t.matmul(PB[0:64, h * 64:(h + 1) * 64], bsall[:, h * 64:(h + 1) * 64],
                            xb[:, h * 64:(h + 1) * 64], start=first, stop=last),
                   waits=[("vv", vv_bs)] if h == 0 else (), inc="te" if h == NH - 1 else None)
        te_ds = R.val("te")
        R.emit("vector", lambda v, ch=ch: v.tensor_tensor(
            s_run[:].rearrange("p (h n) -> p h n", h=NH),
            s_run[:].rearrange("p (h n) -> p h n", h=NH),
            eQa[:, ch * NH:(ch + 1) * NH].rearrange("p (h o) -> p h o", o=1).to_broadcast([64, NH, 64]),
            op=ALU.mult),
            waits=[("te", te_ds)], reads=("s_run", "eQa"), writes=("s_run",))
        R.emit("vector", lambda v: v.tensor_add(s_run[:], s_run[:], PB[0:64, 0:NH * 64]), inc="vv",
               reads=("s_run", "PB"), writes=("s_run",))

    for ch in range(NCH):
        emit_states(ch, load_xt(ch))
    vv_sfin = R.val("vv")

    if cut("p2a"):
        return finale()

    w_cci = R.dma("gpsimd", lambda g: g.dma_start(out=cc_in[:, :], in_=s_run[:]),
                  waits=[("vv", vv_sfin)], stream="cci", K=1)
    R.emit("gpsimd", lambda g: g.collective_compute(
        "AllGather", ALU.bypass, replica_groups=[list(range(n_cores))],
        ins=[cc_in[:]], outs=[cc_out[:]]),
        waits=[w_cci], inc="cc")

    if cut("ag"):
        return finale()

    for bi in range(NZB):
        emit_inproj("z", 1 + NCB + bi, bi, bi * 128, 128, 128)
    F_z = R.fence("wz")

    if cut("p1z"):
        return finale()

    slot_vv = {}
    for r in range(n_cores):
        wsl = R.dma("gpsimd", lambda g, r=r: g.dma_start(
            out=sslot[r % 2][:], in_=cc_out[r * 64:(r + 1) * 64, :]),
            waits=[("cc", 1), ("vv", slot_vv.get(r - 2, 0))] +
                  (([w_cum] + R.fence("wxq")) if r == 0 else []),
            stream="lsl")
        if r == 0:
            R.emit("vector", lambda v, r=r: v.tensor_scalar(
                s_run[:], sslot[r % 2][:], sel[:, r:r + 1], None, op0=ALU.mult),
                waits=[wsl], inc="vv", reads=(f"slots{r % 2}", "sel"), writes=("s_run",))
        else:
            R.emit("vector", lambda v, r=r: v.scalar_tensor_tensor(
                s_run[:], sslot[r % 2][:], sel[:, r:r + 1], s_run[:],
                op0=ALU.mult, op1=ALU.add),
                waits=[wsl], inc="vv",
                reads=(f"slots{r % 2}", "sel", "s_run"), writes=("s_run",))
        slot_vv[r] = R.val("vv")

    if cut("recv"):
        return finale()

    def p2m_chunk(ch):
        xb = XT[ch % 2]
        bb = BC2[ch % 2]
        w_xt = load_xt(ch)
        w_bc = R.dma("gpsimd", lambda g, bb=bb, ch=ch: g.dma_start(
            out=bb[:].rearrange("p (gg w) -> p gg w", gg=2),
            in_=xconv_hbm[D:D + 128, ch * 128:(ch + 1) * 128].rearrange("(gg p) w -> p gg w", gg=2)),
            waits=R.fence("wcv") + [("te", R.val("te"))], stream="lbc")
        w_cr = []
        for half in range(2):
            wpre = R.dma("gpsimd", lambda g, half=half: g.dma_start(
                out=cumrep[half][:, 0:NHH * 128].rearrange("p (h w) -> p h w", w=128),
                in_=causneg_in.unsqueeze(1).to_broadcast([128, NHH, 128])),
                waits=[("ss", R.val("ss")), ("te", R.val("te")), w_cum] + (F_z if ch == 0 else []),
                stream="lcr")
            w_acc = R.dma("gpsimd", lambda g, ch=ch, half=half: g.dma_start(
                out=cumrep[half][:, 0:NHH * 128].rearrange("p (h w) -> p h w", w=128),
                in_=cum_hbm[half * NHH:(half + 1) * NHH, ch * 128:(ch + 1) * 128]
                    .partition_broadcast(128),
                accum_op=ALU.add),
                waits=[wpre], stream="lcr")
            w_cr.append(w_acc)
        w_sz = R.dma("gpsimd", lambda g, ch=ch: g.dma_start(
            out=szc[:].rearrange("p (cc w) -> p cc w", w=128),
            in_=siluz_hbm[:, ch * 128:(ch + 1) * 128].rearrange("(cc p) w -> p cc w", p=128)),
            waits=F_z + [("vv", R.val("vv"))], stream="lsz")

        R.emit("vector", lambda v: v.tensor_copy(s_bf[:], s_run[:]),
               waits=[("te", R.val("te"))], inc="vv", reads=("s_run",), writes=("s_bf",))
        vv_sbf = R.val("vv")

        R.emit("tensor", lambda t, bb=bb: t.matmul(PB[:, 0:128], bb[:, 0:128], bb[:, 128:256],
                                                   start=True, stop=True),
               waits=[w_bc, ("vv", vv_sbf)], inc="te")
        te_g = R.val("te")
        R.emit("vector", lambda v: v.tensor_mul(gmask[:], PB[:, 0:128], caus[:]),
               waits=[("te", te_g)], inc="vv", reads=("PB", "caus"), writes=("gmask",))
        vv_gm = R.val("vv")

        for h in range(NH):
            half, hh = divmod(h, NHH)
            R.emit("scalar", lambda s, h=h, half=half, hh=hh, ch=ch: s.activation(
                dall[:, h * 128:(h + 1) * 128], cumrep[half][:, hh * 128:(hh + 1) * 128],
                AF.Exp, bias=TTa[:, ch * NH + h:ch * NH + h + 1], scale=1.0),
                waits=(w_cr + [("vv", vv_sbf)] + (F_z if ch == 0 else [])) if h == 0 else (),
                inc="ss" if h == NH - 1 else None)
        ss_d = R.val("ss")
        R.emit("vector", lambda v: v.tensor_tensor(
            mall[:].rearrange("p (h w) -> p h w", w=128),
            dall[:].rearrange("p (h w) -> p h w", w=128),
            gmask[:].rearrange("p (o w) -> p o w", o=1).to_broadcast([128, NH, 128]),
            op=ALU.mult),
            waits=[("ss", ss_d)] + (F_z if ch == 0 else []), inc="vv",
            reads=("dall", "gmask"), writes=("mall",))
        vv_m = R.val("vv")

        for h in range(NH):
            first = (h % 8 == 0)
            last = (h % 8 == 7) or h == NH - 1
            R.emit("tensor", lambda t, h=h, xb=xb, first=first, last=last:
                   t.matmul(PA[:, h * 64:(h + 1) * 64], mall[:, h * 128:(h + 1) * 128],
                            xb[:, h * 64:(h + 1) * 64], start=first, stop=last),
                   waits=[("vv", vv_m), w_xt] if h == 0 else (),
                   inc="te" if h == NH - 1 else None)
        te_intra = R.val("te")

        for pr in range(NPAIR):
            first = (pr % 4 == 0)
            last = (pr % 4 == 3) or pr == NPAIR - 1
            R.emit("tensor", lambda t, pr=pr, bb=bb, first=first, last=last:
                   t.matmul(PB[:, pr * 128:(pr + 1) * 128], bb[:, 128:256],
                            s_bf[:, pr * 128:(pr + 1) * 128], start=first, stop=last),
                   waits=[("vv", vv_gm)] if pr == 0 else (), inc="te" if pr == NPAIR - 1 else None)
        te_inter = R.val("te")

        R.emit("vector", lambda v, xb=xb: v.tensor_mul(xd[:], xb[:, 0:D], drep_bf[:]),
               waits=[w_xt], reads=(f"XT{ch % 2}", "drep_bf"), writes=("xd",))
        R.emit("vector", lambda v, ch=ch: v.tensor_tensor(
            t1[:, 0:D].rearrange("p (h n) -> p h n", h=NH),
            PB[:, 0:D].rearrange("p (h n) -> p h n", h=NH),
            ecT[:, ch * NH:(ch + 1) * NH].rearrange("p (h o) -> p h o", o=1).to_broadcast([128, NH, 64]),
            op=ALU.mult),
            waits=[("te", te_inter)], inc="vv", reads=("PB", "ecT"), writes=("t1",))
        vv_t1 = R.val("vv")
        R.emit("vector", lambda v: v.tensor_add(t2[:], t1[:, 0:D], xd[:]),
               reads=("t1", "xd"), writes=("t2",))
        R.emit("vector", lambda v: v.scalar_tensor_tensor(y0[:], PA[:, 0:D], 1.0, t2[:],
                                                          op0=ALU.mult, op1=ALU.add),
               waits=[("te", te_intra)] + R.fence("wyt"), inc="vv",
               reads=("PA", "t2"), writes=("y0",))
        vv_y0 = R.val("vv")

        R.emit("vector", lambda v, xb=xb, ch=ch: v.tensor_tensor(
            bsall[:].rearrange("p (h n) -> p h n", h=NH),
            xb[:, D:D + 64].rearrange("p (o n) -> p o n", o=1).to_broadcast([128, NH, 64]),
            scolT[:, ch * NH:(ch + 1) * NH].rearrange("p (h o) -> p h o", o=1).to_broadcast([128, NH, 64]),
            op=ALU.mult), inc="vv", reads=(f"XT{ch % 2}", "scolT"), writes=("bsall",))
        vv_bs = R.val("vv")
        for h in range(NH):
            first = (h % 8 == 0)
            last = (h % 8 == 7) or h == NH - 1
            R.emit("tensor", lambda t, h=h, xb=xb, first=first, last=last:
                   t.matmul(PB[0:64, h * 64:(h + 1) * 64], bsall[:, h * 64:(h + 1) * 64],
                            xb[:, h * 64:(h + 1) * 64], start=first, stop=last),
                   waits=[("vv", vv_bs), ("vv", vv_t1)] if h == 0 else (),
                   inc="te" if h == NH - 1 else None)
        te_ds = R.val("te")

        w_yt = []
        for cb in range(NCT):
            w_yt.append(R.dma("sync", lambda sy, cb=cb, ch=ch: sy.dma_start_transpose(
                ych[:, cb * T + ch * 128: cb * T + (ch + 1) * 128],
                y0[:, cb * 128:(cb + 1) * 128]),
                waits=[("vv", vv_y0)], stream="wyt"))
        for cb in range(NCT):
            R.emit("vector", lambda v, cb=cb, ch=ch: v.tensor_mul(
                ych[:, cb * T + ch * 128: cb * T + (ch + 1) * 128],
                ych[:, cb * T + ch * 128: cb * T + (ch + 1) * 128],
                szc[:, cb * 128:(cb + 1) * 128]),
                waits=([w_sz] + w_yt) if cb == 0 else (),
                inc="vv" if cb == NCT - 1 else None,
                reads=(f"ych{cb}", "szc"), writes=(f"ych{cb}",))

        R.emit("vector", lambda v, ch=ch: v.tensor_tensor(
            s_run[:].rearrange("p (h n) -> p h n", h=NH),
            s_run[:].rearrange("p (h n) -> p h n", h=NH),
            eQa[:, ch * NH:(ch + 1) * NH].rearrange("p (h o) -> p h o", o=1).to_broadcast([64, NH, 64]),
            op=ALU.mult),
            waits=[("te", te_ds)], reads=("s_run", "eQa"), writes=("s_run",))
        R.emit("vector", lambda v: v.tensor_add(s_run[:], s_run[:], PB[0:64, 0:NH * 64]), inc="vv",
               reads=("s_run", "PB"), writes=("s_run",))

    for ch in range(NCH):
        p2m_chunk(ch)
    vv_p2 = R.val("vv")
    F_yt = R.fence("wyt")

    if cut("p2m"):
        return finale()

    for cb in range(NCT):
        R.dma("sync", lambda sy, cb=cb: sy.dma_start_transpose(
            xq_ch[:, cb * T:(cb + 1) * T], xq_hbm[0:T, cb * 128:(cb + 1) * 128]),
            waits=R.fence("wxq") + [("vv", vv_p2)] + F_yt, stream="lxq")
    F_xq = R.fence("lxq")

    p3_vv_end = {}
    for osi in range(c["NOS"]):
        wo = wos[osi % 2]
        wq = wqs[osi % 2]
        w_wo = R.dma("gpsimd", lambda g, wo=wo, osi=osi: g.dma_start(
            out=wo[:, 0:NCT * OSW].rearrange("p (cc w) -> p cc w", w=OSW),
            in_=w_outT[:, osi * OSW:(osi + 1) * OSW].rearrange("(cc p) w -> p cc w", p=128)),
            waits=[("te", R.val("te")), ("vv", vv_p2)], stream="lwo")
        w_wq = R.dma("gpsimd", lambda g, wq=wq, osi=osi: g.dma_start(
            out=wq[:, 0:NCT * OSW].rearrange("p (cc w) -> p cc w", w=OSW),
            in_=wq_hbm[:, osi * OSW:(osi + 1) * OSW].rearrange("(cc p) w -> p cc w", p=128)),
            waits=F_wq, stream="lwo")
        for ti in range(NT):
            it = osi * NT + ti
            ubuf = uts[it % 4]
            obuf = outs[it % 4]
            w_u = R.dma("gpsimd", lambda g, ubuf=ubuf, ti=ti, osi=osi: g.dma_start(
                out=ubuf[:], in_=u_loc[ti * 128:(ti + 1) * 128, osi * OSW:(osi + 1) * OSW]),
                waits=[("vv", p3_vv_end.get(it - 4, vv_p2))], stream="lu3", K=4)
            pv = PA[:, (it % 4) * 512:(it % 4) * 512 + OSW]
            for ci in range(NCT):
                wl = ([w_wo, w_wq, ("vv", R.val("vv"))] + F_xq) if ci == 0 else []
                R.emit("tensor", lambda t, pv=pv, ci=ci, ti=ti, wo=wo:
                       t.matmul(pv, ych[:, ci * T + ti * 128:ci * T + (ti + 1) * 128],
                                wo[:, ci * OSW:(ci + 1) * OSW], start=(ci == 0), stop=False),
                       waits=wl)
            for ci in range(NCT):
                R.emit("tensor", lambda t, pv=pv, ci=ci, ti=ti, wq=wq:
                       t.matmul(pv, xq_ch[:, ci * T + ti * 128:ci * T + (ti + 1) * 128],
                                wq[:, ci * OSW:(ci + 1) * OSW], start=False, stop=(ci == NCT - 1)),
                       inc="te" if ci == NCT - 1 else None)
            te_t = R.val("te")
            R.emit("vector", lambda v, pv=pv, ubuf=ubuf, obuf=obuf: v.scalar_tensor_tensor(
                obuf[:], pv, 1.0, ubuf[:], op0=ALU.mult, op1=ALU.add),
                waits=[("te", te_t), w_u] + R.fence("wout"), inc="vv",
                reads=("PA", f"uts{it % 4}"), writes=(f"outs{it % 4}",))
            p3_vv_end[it] = R.val("vv")
            R.dma("scalar", lambda s, obuf=obuf, ti=ti, osi=osi: s.dma_start(
                out=out_loc[ti * 128:(ti + 1) * 128, osi * OSW:(osi + 1) * OSW], in_=obuf[:]),
                waits=[("vv", p3_vv_end[it])], stream="wout", K=4)

    return finale()


def make_core_inputs(c, u, conv_w, conv_b, A_log, D_param, dt_bias):
    D, T, NH, NCB = c["D"], c["T"], c["NH"], c["NCB"]
    n_cores = c["n_cores"]
    cw = np.ascontiguousarray(conv_w.reshape(NCB, 128, 4).transpose(1, 0, 2).reshape(128, NCB * 4))
    cb = np.ascontiguousarray(conv_b.reshape(NCB, 128).T)
    dtb = np.ascontiguousarray(dt_bias[:, None])
    alg = np.ascontiguousarray(A_log[:, None])
    drep = np.ascontiguousarray(np.repeat(D_param, 64)[None, :])
    caus = np.triu(np.ones((128, 128), np.float32))
    causneg = np.where(caus > 0, 0.0, -60000.0).astype(np.float32)
    eye = np.eye(128, dtype=np.float32)
    ones1 = np.ones((1, 128), np.float32)
    onecol = np.ones((128, 1), np.float32)
    zer32 = np.zeros((NH, 128), np.float32)
    zpad = np.zeros((16, D), BF)
    cvals = np.zeros((128, 8), np.float32)
    cvals[:, 0] = 1e-5
    cvals[:, 1] = -0.5
    cvals[:, 2] = -1.0
    cvals[:, 3] = 1.0
    maps = []
    for k in range(n_cores):
        b, half = k // 2, k % 2
        u_b = u[b]
        u_slice = np.ascontiguousarray(u_b[half * T:(half + 1) * T])
        halo = np.zeros((128, D), np.float32)
        if half == 1:
            halo[0:3] = u_b[T - 3:T]
        selv = np.zeros((64, n_cores), np.float32)
        if half == 1:
            selv[:, k - 1] = 1.0
        maps.append(dict(
            u_loc=u_slice, u_halo=halo,
            conv_w_l=cw, conv_b_l=cb, dt_bias_c=dtb, a_log_c=alg, d_rep=drep,
            sel_in=selv, caus_in=caus, causneg_in=causneg, eye_in=eye, ones_in=ones1,
            onecol_in=onecol, zer32_in=zer32, zpad_in=zpad, cvals_in=cvals,
        ))
    return maps


_NC_CACHE = {}


def kernel(L0_attn_u, L0_attn_W_in, L0_attn_conv_w, L0_attn_conv_b, L0_attn_W_out,
           L0_attn_A_log, L0_attn_D_param, L0_attn_dt_bias, L0_ffn_weight):
    c = mkcfg(D=2048, T=2048, n_cores=8)
    u = np.ascontiguousarray(np.asarray(L0_attn_u, dtype=np.float32))
    maps = make_core_inputs(
        c, u,
        np.ascontiguousarray(np.asarray(L0_attn_conv_w, np.float32).reshape(c["CONV"], 4)),
        np.ascontiguousarray(np.asarray(L0_attn_conv_b, np.float32)),
        np.ascontiguousarray(np.asarray(L0_attn_A_log, np.float32)),
        np.ascontiguousarray(np.asarray(L0_attn_D_param, np.float32)),
        np.ascontiguousarray(np.asarray(L0_attn_dt_bias, np.float32)),
    )
    w_inT = np.ascontiguousarray(np.asarray(L0_attn_W_in, np.float32).T)
    w_outT = np.ascontiguousarray(np.asarray(L0_attn_W_out, np.float32).T)
    w_ffnT = np.ascontiguousarray(np.asarray(L0_ffn_weight, np.float32).T)
    for m in maps:
        m["w_inT"] = w_inT
        m["w_outT"] = w_outT
        m["w_ffnT"] = w_ffnT
    key = (c["D"], c["T"], c["n_cores"])
    if key not in _NC_CACHE:
        _NC_CACHE[key] = build_nc(c)
    nc = _NC_CACHE[key]
    res = run_bass_kernel_spmd(nc, maps, core_ids=list(range(c["n_cores"])))
    B, L = u.shape[0], u.shape[1]
    out = np.empty((B, L, c["D"]), np.float32)
    for k in range(c["n_cores"]):
        b, half = k // 2, k % 2
        out[b, half * c["T"]:(half + 1) * c["T"], :] = res.results[k]["out_loc"]
    return out


# version 03
# speedup vs baseline: 1.1053x; 1.1053x over previous
import contextlib
import numpy as np
import ml_dtypes
import concourse.bass as bass
import concourse.mybir as mybir
from concourse.bass_utils import run_bass_kernel_spmd

F32 = mybir.dt.float32
BF16 = mybir.dt.bfloat16
I32 = mybir.dt.int32
AF = mybir.ActivationFunctionType
ALU = mybir.AluOpType
BF = ml_dtypes.bfloat16

MAGIC = float(3 * 2.0 ** 22)
PIN = 384.5
PINI = 384.0
ANDMASK = 0xFFFF8000 - (1 << 32)


def mkcfg(D, T, n_cores):
    NH = D // 64
    c = dict(
        D=D, T=T, n_cores=n_cores, NH=NH, NPAIR=NH // 2,
        CONV=D + 128, NCB=D // 128 + 1, NZB=D // 128,
        DIP=2 * D + 128 + NH, NCH=T // 128, NT=T // 128,
        TH=T + 3,
    )
    c["XNS"] = ((c["TH"] + 15) // 16) * 16
    strips = []
    off = 0
    while off < c["TH"]:
        w = min(512, c["TH"] - off)
        strips.append((off, w))
        off += w
    c["XBC_STRIPS"] = strips
    zstrips = []
    off = 3
    while off < c["TH"]:
        w = min(512, c["TH"] - off)
        zstrips.append((off, w))
        off += w
    c["Z_STRIPS"] = zstrips
    c["OSW"] = min(256, D)
    c["NOS"] = D // c["OSW"]
    return c


class Rec:

    ENGINES = ("gpsimd", "sync", "scalar", "vector", "tensor")

    def __init__(self):
        self.ops = {e: [] for e in self.ENGINES}
        self.ctr = {s: 0 for s in ("te", "vv", "ss", "cc")}
        self.streams = {}
        self._vw = set()
        self._spacer_fn = None
        self.n_spacers = 0

    def _sem(self, name):
        if name not in self.ctr:
            self.ctr[name] = 0
        return name

    def emit(self, eng, fn, waits=(), inc=None, inc_amt=1, reads=None, writes=None):
        if eng == "vector":
            rd = {"*"} if reads is None else set(reads)
            wr = {"*"} if writes is None else set(writes)
            prev = self._vw
            if prev and rd and ("*" in prev or "*" in rd or (rd & prev)):
                assert self._spacer_fn is not None
                self.ops["vector"].append((self._spacer_fn, [], None))
                self.n_spacers += 1
            self._vw = wr
        w = []
        for s, v in waits:
            if v > 0:
                w.append((s, v))
        i = None
        if inc is not None:
            self._sem(inc)
            self.ctr[inc] += inc_amt
            i = (inc, inc_amt)
        self.ops[eng].append((fn, w, i))

    def dma(self, eng, fn, waits=(), stream="misc", K=2):
        st = self.streams.setdefault(stream, {"n": 0, "K": K})
        assert st["K"] == K, f"stream {stream} K mismatch"
        n = st["n"]
        slot = n % K
        sem = self._sem(f"d_{stream}_{slot}")
        prior = 16 * (n // K)
        self.emit(eng, fn, list(waits) + [(sem, prior)], inc=sem, inc_amt=16)
        st["n"] = n + 1
        return (sem, 16 * (n // K + 1))

    def fence(self, *streams):
        out = []
        names = streams if streams else list(self.streams.keys())
        for nm in names:
            st = self.streams.get(nm)
            if not st:
                continue
            K = st["K"]
            for slot in range(K):
                cnt = (st["n"] + K - 1 - slot) // K
                if cnt > 0:
                    out.append((f"d_{nm}_{slot}", 16 * cnt))
        return out

    def val(self, sem):
        return self.ctr[sem]

    def replay(self, eng_name, eng, sems):
        last = {}
        for fn, waits, inc in self.ops[eng_name]:
            for s, v in waits:
                if last.get(s, 0) < v:
                    eng.wait_ge(sems[s], v)
                    last[s] = v
            inst = fn(eng)
            if inc is not None:
                s, amt = inc
                inst.then_inc(sems[s], amt)


def _finish(nc, es, R):
    sems = {name: es.enter_context(nc.semaphore(name)) for name in R.ctr}
    block = es.enter_context(nc.Block())

    @block.gpsimd
    def _(g):
        R.replay("gpsimd", g, sems)

    @block.sync
    def _(sy):
        R.replay("sync", sy, sems)

    @block.scalar
    def _(s):
        R.replay("scalar", s, sems)

    @block.vector
    def _(v):
        R.replay("vector", v, sems)

    @block.tensor
    def _(t):
        R.replay("tensor", t, sems)

    es.close()
    return nc


def build_nc(c, detect_races=False, upto=None, dbg=False):
    D, T, NH = c["D"], c["T"], c["NH"]
    NPAIR, CONV, NCB, NZB = c["NPAIR"], c["CONV"], c["NCB"], c["NZB"]
    NCH, NT, TH, XNS = c["NCH"], c["NT"], c["TH"], c["XNS"]
    n_cores = c["n_cores"]
    NCT = D // 128
    NHH = NH // 2
    OSW = c["OSW"]
    STKB = 32

    nc = bass.Bass("TRN2", target_bir_lowering=False, debug=False,
                   detect_race_conditions=detect_races)

    def din(name, shape, dt=F32):
        return nc.dram_tensor(name, shape, dt, kind="ExternalInput").ap()

    u_loc = din("u_loc", [T, D])
    u_halo = din("u_halo", [128, D])
    w_inT = din("w_inT", [D, c["DIP"]])
    w_outT = din("w_outT", [D, D])
    w_ffnT = din("w_ffnT", [D, D])
    conv_w_l = din("conv_w_l", [128, NCB * 4])
    conv_b_l = din("conv_b_l", [128, NCB])
    dt_bias_c = din("dt_bias_c", [NH, 1])
    a_log_c = din("a_log_c", [NH, 1])
    d_rep = din("d_rep", [1, D])
    sel_in = din("sel_in", [64, n_cores])
    caus_in = din("caus_in", [128, 128])
    causneg_in = din("causneg_in", [128, 128])
    eye_in = din("eye_in", [128, 128])
    ones_in = din("ones_in", [1, 128])
    onecol_in = din("onecol_in", [128, 1])
    zer32_in = din("zer32_in", [NH, 128])
    zpad_in = din("zpad_in", [16, D], BF16)
    cvals_in = din("cvals_in", [128, 8])

    out_loc = nc.dram_tensor("out_loc", [T, D], F32, kind="ExternalOutput").ap()

    kd = dict(kind="ExternalOutput") if dbg else {}
    xn_hbm = nc.dram_tensor("xn_hbm", [XNS, D], BF16, **kd).ap()
    xq_hbm = nc.dram_tensor("xq_hbm", [T, D], BF16, **kd).ap()
    xconv_hbm = nc.dram_tensor("xconv_hbm", [CONV, T], BF16, **kd).ap()
    siluz_hbm = nc.dram_tensor("siluz_hbm", [D, T], BF16, **kd).ap()
    cum_hbm = nc.dram_tensor("cum_hbm", [NH, T], F32, **kd).ap()
    wq_hbm = nc.dram_tensor("wq_hbm", [D, D], BF16, **kd).ap()
    ws_hbm = nc.dram_tensor("ws_hbm", [1, 3], F32).ap()
    cc_in = nc.dram_tensor("cc_in", [64, NH * 64], F32).ap()
    cc_out = nc.dram_tensor("cc_out", [64 * n_cores, NH * 64], F32,
                            addr_space="Shared" if n_cores > 4 else "Local").ap()

    es = contextlib.ExitStack()
    _auto = []

    def sbuf(name, shape, dt=F32):
        h = es.enter_context(nc.sbuf_tensor(name, shape, dt))
        _auto.append((h, shape, dt))
        return h

    SBUF_CAP = 229344

    def tsize(shape, dt):
        n = 1
        for s in shape[1:]:
            n *= s
        return ((n * mybir.dt.size(dt) + 63) // 64) * 64

    specA, specC = {}, {}

    def chain(spec, items):
        cur = 0
        for nm, shp, dt in items:
            spec[nm] = (shp, dt, cur)
            cur += tsize(shp, dt)
        return cur

    wA0 = chain(specA, [("ub0", [128, D], F32), ("ub1", [128, D], F32),
                        ("xnf", [128, D], F32), ("xnb", [128, D], BF16),
                        ("qt", [128, D], F32), ("xqb", [128, D], BF16),
                        ("wf0", [128, D], F32), ("wf1", [128, D], F32),
                        ("qsq", [128, D], F32)])
    wA1 = chain(specA, [("sc0", [32, T], F32), ("sc1", [32, T], F32),
                        ("sc2", [32, T], F32), ("sc3", [32, T], F32),
                        ("stk", [3 * STKB, T], F32)])
    wA2 = chain(specA, [("XT0", [128, CONV], BF16), ("XT1", [128, CONV], BF16),
                        ("bsall", [128, NH * 64], BF16), ("BC2a", [64, 256], BF16),
                        ("BC2b", [64, 256], BF16), ("gmask", [128, 128], BF16),
                        ("szc", [128, D], BF16), ("y0", [128, D], BF16),
                        ("slots0", [64, NH * 64], F32), ("slots1", [64, NH * 64], F32)])
    specA["xq_ch"] = ([128, NCT * T], BF16, 0)
    widthA = max(wA0, wA1, wA2, tsize([128, NCT * T], BF16))

    widthB = max(tsize([128, NCT * XNS], BF16), tsize([128, NCT * T], BF16))

    wC0 = chain(specC, [("wstage0", [128, NCT * 128], BF16), ("wstage1", [128, NCT * 128], BF16),
                        ("stage", [128, TH], BF16), ("convacc", [128, T], F32),
                        ("xcb", [128, T], BF16), ("dt_toks", [NH, T], F32)])
    wC1 = chain(specC, [("cumrep0", [128, NHH * 128], F32), ("cumrep1", [128, NHH * 128], F32),
                        ("dall", [128, NH * 128], BF16), ("mall", [128, NH * 128], BF16),
                        ("t1", [128, D], F32), ("xd", [128, D], BF16), ("t2", [128, D], BF16)])
    wC2 = chain(specC, [("wos0", [128, NCT * OSW], BF16), ("wos1", [128, NCT * OSW], BF16),
                        ("wqs0", [128, NCT * OSW], BF16), ("wqs1", [128, NCT * OSW], BF16),
                        ("uts0", [128, OSW], F32), ("uts1", [128, OSW], F32),
                        ("uts2", [128, OSW], F32), ("uts3", [128, OSW], F32),
                        ("outs0", [128, OSW], F32), ("outs1", [128, OSW], F32),
                        ("outs2", [128, OSW], F32), ("outs3", [128, OSW], F32)])
    widthC = max(wC0, wC1, wC2)

    ARENA = widthA + widthB + widthC
    A0 = SBUF_CAP - ARENA - 1024
    offB, offC = widthA, widthA + widthB

    def at(spec, base, nm):
        shp, dt, off = spec[nm]
        return nc.alloc_sbuf_tensor_at(nm, shp, dt, offset=A0 + base + off).ap()

    caus = sbuf("caus", [128, 128])
    eye = sbuf("eye", [128, 128])
    ones1 = sbuf("ones1", [1, 128])
    onecol = sbuf("onecol", [128, 1])
    zer32 = sbuf("zer32", [NH, 128])
    cwsb = sbuf("cwsb", [128, NCB * 4])
    cbsb = sbuf("cbsb", [128, NCB])
    dtb = sbuf("dtb", [NH, 1])
    alog = sbuf("alog", [NH, 1])
    drep_bf = sbuf("drep_bf", [128, D], BF16)
    sel = sbuf("sel", [64, n_cores])
    cvals = sbuf("cvals", [128, 8])
    mv = sbuf("mv", [128, 2])
    lnt = sbuf("lnt", [128, 4])
    amx = sbuf("amx", [128, 4])
    wsacc = sbuf("wsacc", [128, 1])
    wscol2 = sbuf("wscol2", [128, 2])
    ws11 = sbuf("ws11", [1, 4])
    wsrow = sbuf("wsrow", [1, 128])
    TTa = sbuf("TTa", [128, NCH * NH])
    ecT = sbuf("ecT", [128, NCH * NH])
    scolT = sbuf("scolT", [128, NCH * NH], BF16)
    eQa = sbuf("eQa", [64, NCH * NH])
    cqr = sbuf("cqr", [1, NH])
    s_run = sbuf("s_run", [64, NH * 64])
    s_bf = sbuf("s_bf", [64, NH * 64], BF16)
    spc = sbuf("spc", [128, 1])

    stack_top = max(nc.lookup_mloc(h).addr + tsize(shp, dt) for h, shp, dt in _auto)
    assert stack_top <= A0, f"left stack {stack_top} overlaps arena base {A0}"

    ub = [at(specA, 0, "ub0"), at(specA, 0, "ub1")]
    xnf = at(specA, 0, "xnf")
    xnb = at(specA, 0, "xnb")
    qt = at(specA, 0, "qt")
    xqb = at(specA, 0, "xqb")
    wf = [at(specA, 0, "wf0"), at(specA, 0, "wf1")]
    qsq = at(specA, 0, "qsq")
    wqt = qt
    wqb = xqb
    wabs = xnf
    sc_ = [at(specA, 0, f"sc{i}") for i in range(4)]
    stk = at(specA, 0, "stk")
    XT = [at(specA, 0, "XT0"), at(specA, 0, "XT1")]
    bsall = at(specA, 0, "bsall")
    BC2 = [at(specA, 0, "BC2a"), at(specA, 0, "BC2b")]
    gmask = at(specA, 0, "gmask")
    szc = at(specA, 0, "szc")
    y0 = at(specA, 0, "y0")
    sslot = [at(specA, 0, "slots0"), at(specA, 0, "slots1")]
    xq_ch = at(specA, 0, "xq_ch")
    xn_ch = nc.alloc_sbuf_tensor_at("xn_ch", [128, NCT * XNS], BF16, offset=A0 + offB).ap()
    ych = nc.alloc_sbuf_tensor_at("ych", [128, NCT * T], BF16, offset=A0 + offB).ap()
    wstage = [at(specC, offC, "wstage0"), at(specC, offC, "wstage1")]
    stage = at(specC, offC, "stage")
    convacc = at(specC, offC, "convacc")
    xcb = at(specC, offC, "xcb")
    dt_toks = at(specC, offC, "dt_toks")
    cumrep = [at(specC, offC, "cumrep0"), at(specC, offC, "cumrep1")]
    dall = at(specC, offC, "dall")
    mall = at(specC, offC, "mall")
    t1 = at(specC, offC, "t1")
    xd = at(specC, offC, "xd")
    t2 = at(specC, offC, "t2")
    wos = [at(specC, offC, "wos0"), at(specC, offC, "wos1")]
    wqs = [at(specC, offC, "wqs0"), at(specC, offC, "wqs1")]
    uts = [at(specC, offC, f"uts{i}") for i in range(4)]
    outs = [at(specC, offC, f"outs{i}") for i in range(4)]

    PA = es.enter_context(nc.psum_tensor("PA", [128, 2048], F32))
    PB = es.enter_context(nc.psum_tensor("PB", [128, 2048], F32))

    R = Rec()
    R._spacer_fn = lambda v: v.tensor_copy(spc[:], onecol[:])

    def cut(name):
        return upto is not None and upto == name

    def finale():
        R.emit("vector", lambda v: v.engine_nop(),
               waits=R.fence() + [("te", R.val("te")), ("ss", R.val("ss"))],
               reads=(), writes=())
        return _finish(nc, es, R)

    for dst, src in ((caus, caus_in), (eye, eye_in), (ones1, ones_in),
                     (onecol, onecol_in), (zer32, zer32_in), (cwsb, conv_w_l),
                     (cbsb, conv_b_l), (dtb, dt_bias_c), (alog, a_log_c),
                     (sel, sel_in), (cvals, cvals_in)):
        R.dma("gpsimd", lambda g, dst=dst, src=src: g.dma_start(out=dst[:], in_=src), stream="cst")
    R.dma("gpsimd", lambda g: g.dma_start(out=drep_bf[:], in_=d_rep[0, :].partition_broadcast(128)),
          stream="cst")
    if XNS > TH:
        R.dma("gpsimd", lambda g: g.dma_start(out=xn_hbm[TH:XNS, :], in_=zpad_in[0:XNS - TH, :]),
              stream="cst")
    F_cst = R.fence("cst")
    for e in Rec.ENGINES:
        R.emit(e, lambda eng: None, waits=F_cst)

    p0_vv_end = {}
    p0_ss_end = {}
    p0_vv_mv = {}

    def p0_tile(i, halo):
        buf = ub[(i + 1) % 2 if halo else i % 2]
        src = u_halo if halo else u_loc[i * 128:(i + 1) * 128, :]
        key = -1 if halo else i
        wu = R.dma("gpsimd", lambda g, buf=buf, src=src: g.dma_start(out=buf[:], in_=src),
                   waits=[("vv", p0_vv_end.get(key - 2, 0)), ("ss", p0_ss_end.get(key - 2, 0))],
                   stream="u0")
        w0 = [wu, ("ss", R.val("ss"))] + R.fence("wxn", "wxq")
        bnm = f"ub{(i + 1) % 2 if halo else i % 2}"
        R.emit("scalar", lambda s, buf=buf: s.activation(qsq[:], buf[:], AF.Square),
               waits=[wu, ("vv", p0_vv_mv.get(key - 1, 0))], inc="ss")
        ss_sq = R.val("ss")
        p0_ss_end[key] = ss_sq
        R.emit("vector", lambda v, buf=buf: v.reduce_sum(mv[:, 0:1], buf[:], axis=mybir.AxisListType.X),
               waits=w0, reads=(bnm,), writes=("mv",))
        R.emit("vector", lambda v: v.reduce_sum(mv[:, 1:2], qsq[:], axis=mybir.AxisListType.X),
               waits=[("ss", ss_sq)], reads=("qsq",), writes=("mv",))
        R.emit("vector", lambda v: v.tensor_scalar(mv[:, 0:1], mv[:, 0:1], 1.0 / D, None, op0=ALU.mult),
               reads=("mv",), writes=("mv",))
        R.emit("vector", lambda v: v.tensor_scalar(mv[:, 1:2], mv[:, 1:2], 1.0 / D, None, op0=ALU.mult),
               reads=("mv",), writes=("mv",))
        R.emit("vector", lambda v: v.tensor_mul(lnt[:, 3:4], mv[:, 0:1], mv[:, 0:1]),
               reads=("mv",), writes=("lnt",))
        R.emit("vector", lambda v: v.tensor_sub(mv[:, 1:2], mv[:, 1:2], lnt[:, 3:4]), inc="vv",
               reads=("mv", "lnt"), writes=("mv",))
        vv_mv = R.val("vv")
        p0_vv_mv[key] = vv_mv
        R.emit("scalar", lambda s: s.activation(lnt[:, 0:1], mv[:, 1:2], AF.Ln, bias=cvals[:, 0:1],
                                                scale=1.0),
               waits=[("vv", vv_mv)])
        R.emit("scalar", lambda s: s.activation(lnt[:, 1:2], lnt[:, 0:1], AF.Exp, scale=cvals[:, 1:2]),
               inc="ss")
        ss_r = R.val("ss")
        R.emit("vector", lambda v, buf=buf: v.tensor_scalar(xnf[:], buf[:], mv[:, 0:1], lnt[:, 1:2],
                                                            op0=ALU.subtract, op1=ALU.mult),
               waits=[("ss", ss_r)], inc="vv", reads=(bnm, "mv", "lnt"), writes=("xnf",))
        vv_xnb = R.val("vv")
        R.dma("gpsimd", lambda g, i=i, halo=halo: g.dma_start(
            out=xn_hbm[0:3, :] if halo else xn_hbm[3 + i * 128: 3 + (i + 1) * 128, :],
            in_=xnf[0:3, :] if halo else xnf[:]),
            waits=[("vv", vv_xnb)], stream="wxn")
        if not halo:
            R.emit("vector", lambda v: v.reduce_max(amx[:, 0:1], xnf[:], axis=mybir.AxisListType.X,
                                                    apply_absolute_value=True),
                   reads=("xnf",), writes=("amx",))
            R.emit("vector", lambda v: v.tensor_scalar_max(amx[:, 1:2], amx[:, 0:1], 1e-8), inc="vv",
                   reads=("amx",), writes=("amx",))
            vv_am = R.val("vv")
            R.emit("scalar", lambda s: s.activation(lnt[:, 2:3], amx[:, 1:2], AF.Ln),
                   waits=[("vv", vv_am)])
            R.emit("scalar", lambda s: s.activation(amx[:, 2:3], lnt[:, 2:3], AF.Exp, scale=cvals[:, 2:3]),
                   inc="ss")
            ss_am = R.val("ss")
            R.emit("vector", lambda v: v.tensor_scalar(amx[:, 2:3], amx[:, 2:3], 127.0, None, op0=ALU.mult),
                   waits=[("ss", ss_am)], reads=("amx",), writes=("amx",))
            R.emit("vector", lambda v: v.tensor_scalar(amx[:, 3:4], amx[:, 1:2], 1.0 / 127.0, None,
                                                       op0=ALU.mult), reads=("amx",), writes=("amx",))
            R.emit("vector", lambda v: v.tensor_scalar(qt[:], xnf[:], amx[:, 2:3], MAGIC,
                                                       op0=ALU.mult, op1=ALU.add),
                   reads=("xnf", "amx"), writes=("qt",))
            R.emit("vector", lambda v: v.tensor_scalar(xqb[:], qt[:], MAGIC, amx[:, 3:4],
                                                       op0=ALU.subtract, op1=ALU.mult), inc="vv",
                   reads=("qt", "amx"), writes=("xqb",))
            vv_xq = R.val("vv")
            R.dma("scalar", lambda s, i=i: s.dma_start(out=xq_hbm[i * 128:(i + 1) * 128, :], in_=xqb[:]),
                  waits=[("vv", vv_xq)], stream="wxq")
        p0_vv_end[key] = R.val("vv")

    p0_tile(0, halo=True)
    for i in range(NT):
        p0_tile(i, halo=False)
    F_p0w = R.fence("wxn", "wxq")

    if cut("p0"):
        return finale()

    for cb in range(NCT):
        R.dma("sync", lambda sy, cb=cb: sy.dma_start_transpose(
            xn_ch[:, cb * XNS:(cb + 1) * XNS],
            xn_hbm[0:XNS, cb * 128:(cb + 1) * 128]),
            waits=F_p0w + F_cst, stream="lxn")
    F_xn = R.fence("lxn")

    if cut("xn"):
        return finale()

    def pa_strip(si, width):
        return PA[:, (si % 4) * 512:(si % 4) * 512 + width]

    p1_te_end = {}
    p1_wr_end = {}
    p1_ss_end = {}

    def emit_inproj(kind, key, bi, wcol0, wwidth, outparts):
        buf = wstage[key % 2]
        wws = R.dma("gpsimd", lambda g, buf=buf, wcol0=wcol0, wwidth=wwidth: g.dma_start(
            out=buf[:, 0:NCT * wwidth].rearrange("p (cc w) -> p cc w", w=wwidth),
            in_=w_inT[:, wcol0:wcol0 + wwidth].rearrange("(cc p) w -> p cc w", p=128)),
            waits=[("te", p1_te_end.get(key - 2, 0))], stream="wst")
        strips = c["XBC_STRIPS"] if kind == "xbc" else c["Z_STRIPS"]
        vv_before = R.val("vv")
        ss_before = R.val("ss")
        strip_evac = {}
        for si, (off, width) in enumerate(strips):
            pv = pa_strip(si, width)
            for ci in range(NCT):
                wl = []
                if ci == 0:
                    if si == 0:
                        wl = [wws, ("vv", vv_before), ("ss", ss_before)] + F_xn
                    elif si >= 4:
                        wl = [(("ss" if kind == "z" else "vv"), strip_evac[si - 4])]
                R.emit("tensor", lambda t, pv=pv, buf=buf, ci=ci, off=off, width=width,
                       wwidth=wwidth, outparts=outparts, first=(ci == 0), last=(ci == NCT - 1):
                       t.matmul(pv[0:outparts, :], buf[:, ci * wwidth:(ci + 1) * wwidth],
                                xn_ch[:, ci * XNS + off: ci * XNS + off + width],
                                start=first, stop=last),
                       waits=wl, inc="te" if ci == NCT - 1 else None)
            te_t = R.val("te")
            if kind == "xbc":
                R.emit("vector", lambda v, pv=pv, off=off, width=width:
                       v.tensor_copy(stage[:, off:off + width], pv),
                       waits=[("te", te_t)], inc="vv", reads=("PA",), writes=("stage",))
                strip_evac[si] = R.val("vv")
            elif kind == "z":
                R.emit("scalar", lambda s, pv=pv, off=off, width=width:
                       s.activation(xcb[:, off - 3:off - 3 + width], pv, AF.Silu),
                       waits=[("te", te_t), p1_wr_end.get(key - 1, ("ss", 0))], inc="ss")
                strip_evac[si] = R.val("ss")
            else:
                R.emit("vector", lambda v, pv=pv, off=off, width=width, outparts=outparts:
                       v.tensor_copy(dt_toks[:, off - 3:off - 3 + width], pv[0:outparts, :]),
                       waits=[("te", te_t)], inc="vv", reads=("PA",), writes=("dt_toks",))
        p1_te_end[key] = R.val("te")
        if kind == "xbc":
            R.emit("vector", lambda v, bi=bi:
                   v.tensor_scalar(convacc[:], stage[:, 0:T], cwsb[:, bi * 4:bi * 4 + 1], None,
                                   op0=ALU.mult),
                   waits=[("ss", p1_ss_end.get(key - 1, 0))],
                   reads=("stage", "cwsb"), writes=("convacc",))
            R.emit("vector", lambda v, bi=bi:
                   v.tensor_scalar(t1[:, 0:T], stage[:, 1:1 + T], cwsb[:, bi * 4 + 1:bi * 4 + 2], None,
                                   op0=ALU.mult),
                   reads=("stage", "cwsb"), writes=("cvt",))
            R.emit("vector", lambda v, bi=bi:
                   v.scalar_tensor_tensor(convacc[:], stage[:, 2:2 + T],
                                          cwsb[:, bi * 4 + 2:bi * 4 + 3],
                                          convacc[:], op0=ALU.mult, op1=ALU.add),
                   reads=("stage", "cwsb", "convacc"), writes=("convacc",))
            R.emit("vector", lambda v, bi=bi:
                   v.scalar_tensor_tensor(t1[:, 0:T], stage[:, 3:3 + T],
                                          cwsb[:, bi * 4 + 3:bi * 4 + 4],
                                          t1[:, 0:T], op0=ALU.mult, op1=ALU.add),
                   reads=("stage", "cwsb", "cvt"), writes=("cvt",))
            R.emit("vector", lambda v: v.tensor_add(convacc[:], convacc[:], t1[:, 0:T]), inc="vv",
                   reads=("convacc", "cvt"), writes=("convacc",))
            vv_conv = R.val("vv")
            R.emit("scalar", lambda s, bi=bi:
                   s.activation(xcb[:], convacc[:], AF.Silu, bias=cbsb[:, bi:bi + 1], scale=1.0),
                   waits=[("vv", vv_conv), p1_wr_end.get(key - 1, ("ss", 0))], inc="ss")
            p1_ss_end[key] = R.val("ss")
            p1_wr_end[key] = R.dma("scalar", lambda s, bi=bi: s.dma_start(
                out=xconv_hbm[bi * 128:(bi + 1) * 128, :], in_=xcb[:]),
                waits=[("ss", p1_ss_end[key])], stream="wcv")
        elif kind == "z":
            p1_ss_end[key] = R.val("ss")
            p1_wr_end[key] = R.dma("scalar", lambda s, bi=bi: s.dma_start(
                out=siluz_hbm[bi * 128:(bi + 1) * 128, :], in_=xcb[:]),
                waits=[("ss", p1_ss_end[key])], stream="wz")

    emit_inproj("dt", 0, 0, 2 * D + 128, NH, NH)
    vv_dt = R.val("vv")

    spm = sc_[0][0:NH, :]
    spa = sc_[1][0:NH, :]
    spe = sc_[2][0:NH, :]
    spl = sc_[3][0:NH, :]
    R.emit("vector", lambda v: v.tensor_scalar(spm, dt_toks[:], dtb[:], None, op0=ALU.add),
           waits=[("vv", vv_dt)] + F_cst + F_p0w, inc="vv",
           reads=("dt_toks", "dtb"), writes=("sc0",))
    vv_t = R.val("vv")
    R.emit("scalar", lambda s: s.activation(spa, spm, AF.Abs), waits=[("vv", vv_t)])
    R.emit("scalar", lambda s: s.activation(spe, spa, AF.Exp, scale=cvals[0:NH, 2:3]))
    R.emit("scalar", lambda s: s.activation(spl, spe, AF.Ln, bias=cvals[0:NH, 3:4]),
           waits=R.fence("wxn"), inc="ss")
    ss_t = R.val("ss")
    mx = sc_[2][0:NH, :]
    dtv = sc_[1][0:NH, :]
    logda = sc_[0][0:NH, :]
    logdt = sc_[3][0:NH, :]
    R.emit("vector", lambda v: v.tensor_scalar_max(mx, spm, 0.0), waits=[("ss", ss_t)],
           reads=("sc0",), writes=("sc2",))
    R.emit("vector", lambda v: v.tensor_add(dtv, mx, spl),
           reads=("sc2", "sc3"), writes=("sc1",))
    R.emit("scalar", lambda s: s.activation(lnt[0:NH, 3:4], alog[:], AF.Exp), inc="ss")
    ss_t = R.val("ss")
    R.emit("vector", lambda v: v.tensor_scalar(logda, dtv, lnt[0:NH, 3:4], -1.0,
                                               op0=ALU.mult, op1=ALU.mult),
           waits=[("ss", ss_t)], inc="vv", reads=("sc1", "lnt"), writes=("sc0",))
    vv_t = R.val("vv")
    R.emit("scalar", lambda s: s.activation(logdt, dtv, AF.Ln), waits=[("vv", vv_t)], inc="ss")
    ss_logdt = R.val("ss")
    if NH < STKB:
        R.emit("vector", lambda v: v.memset(stk[:], 0.0), reads=(), writes=("stk",))
    for ch in range(NCH):
        sl = slice(ch * 128, (ch + 1) * 128)
        R.emit("vector", lambda v, sl=sl: v.tensor_tensor_scan(
            stk[0:NH, sl], logda[0:NH, sl], zer32[:, 0:128], 0.0, op0=ALU.add, op1=ALU.add),
            waits=([("ss", ss_logdt)] + R.fence("wxq")) if ch == 0 else (),
            reads=("sc0", "zer32"), writes=("stk",))
        R.emit("vector", lambda v, sl=sl: v.scalar_tensor_tensor(
            stk[STKB:STKB + NH, sl], stk[0:NH, sl], -1.0, logdt[0:NH, sl],
            op0=ALU.mult, op1=ALU.add),
            reads=("stk", "sc3"), writes=("stk",))
        R.emit("vector", lambda v, sl=sl, ch=ch: v.tensor_scalar(
            stk[2 * STKB:2 * STKB + NH, sl], stk[STKB:STKB + NH, sl],
            stk[0:NH, ch * 128 + 127:ch * 128 + 128], None, op0=ALU.add),
            reads=("stk",), writes=("stk",),
            inc="vv" if ch == NCH - 1 else None)
    vv_stk = R.val("vv")
    w_cum = R.dma("scalar", lambda s: s.dma_start(out=cum_hbm[:, :], in_=stk[0:NH, :]),
                  waits=[("vv", vv_stk)], stream="wcm", K=1)
    for ch in range(NCH):
        R.emit("tensor", lambda t, ch=ch: t.transpose(
            PB[:, 1024:1024 + 3 * STKB], stk[:, ch * 128:(ch + 1) * 128], eye[0:3 * STKB, 0:3 * STKB]),
            waits=[("vv", vv_stk if ch == 0 else R.val("vv")), ("ss", R.val("ss"))], inc="te")
        te_t = R.val("te")
        R.emit("vector", lambda v, ch=ch: v.tensor_copy(
            TTa[:, ch * NH:(ch + 1) * NH], PB[:, 1024 + STKB:1024 + STKB + NH]),
            waits=[("te", te_t)], inc="vv", reads=("PB",), writes=("TTa",))
        vv_tt = R.val("vv")
        R.emit("scalar", lambda s, ch=ch: s.activation(
            ecT[:, ch * NH:(ch + 1) * NH], PB[:, 1024:1024 + NH], AF.Exp),
            waits=[("te", te_t)])
        R.emit("scalar", lambda s, ch=ch: s.activation(
            scolT[:, ch * NH:(ch + 1) * NH],
            PB[:, 1024 + 2 * STKB:1024 + 2 * STKB + NH], AF.Exp),
            inc="ss")
        ss_exp = R.val("ss")
        R.emit("tensor", lambda t, ch=ch: t.transpose(
            PB[0:1, 1200:1200 + NH], stk[0:NH, ch * 128 + 127:ch * 128 + 128], eye[0:NH, 0:NH]),
            waits=[("vv", vv_tt), ("ss", ss_exp)], inc="te")
        te_t = R.val("te")
        R.emit("vector", lambda v: v.tensor_copy(cqr[:], PB[0:1, 1200:1200 + NH]),
               waits=[("te", te_t)], inc="vv", reads=("PB",), writes=("cqr",))
        vv_t = R.val("vv")
        R.emit("tensor", lambda t: t.matmul(PB[0:64, 1216:1216 + NH], ones1[:, 0:64], cqr[:],
                                            start=True, stop=True),
               waits=[("vv", vv_t)], inc="te")
        te_t = R.val("te")
        R.emit("scalar", lambda s, ch=ch: s.activation(
            eQa[:, ch * NH:(ch + 1) * NH], PB[0:64, 1216:1216 + NH], AF.Exp),
            waits=[("te", te_t)], inc="ss")
    ss_pre = R.val("ss")

    if cut("pre"):
        return finale()

    NWT = D // 128
    ws_ss_end = {}
    for j in range(NWT):
        b = wf[j % 2]
        wwf = R.dma("gpsimd", lambda g, b=b, j=j: g.dma_start(out=b[:], in_=w_ffnT[j * 128:(j + 1) * 128, :]),
                    waits=[("ss", ws_ss_end.get(j - 2, 0))], stream="wf")
        R.emit("scalar", lambda s, b=b: s.activation(wabs[:], b[:], AF.Abs),
               waits=[wwf, ("vv", R.val("vv"))] + (F_p0w if j == 0 else []), inc="ss")
        ss_t = R.val("ss")
        ws_ss_end[j] = ss_t
        if j == 0:
            R.emit("vector", lambda v: v.reduce_sum(wsacc[:], wabs[:], axis=mybir.AxisListType.X),
                   waits=[("ss", ss_t)], inc="vv", reads=("xnf",), writes=("wsacc",))
        else:
            R.emit("vector", lambda v: v.reduce_sum(lnt[:, 3:4], wabs[:], axis=mybir.AxisListType.X),
                   waits=[("ss", ss_t)], reads=("xnf",), writes=("lnt",))
            R.emit("vector", lambda v: v.tensor_add(wsacc[:], wsacc[:], lnt[:, 3:4]), inc="vv",
                   reads=("wsacc", "lnt"), writes=("wsacc",))
    vv_ws = R.val("vv")
    R.emit("tensor", lambda t: t.transpose(PB[0:1, 0:128], wsacc[:], eye[:, 0:128]),
           waits=[("vv", vv_ws)] + F_cst, inc="te")
    te_ws = R.val("te")
    R.emit("vector", lambda v: v.tensor_copy(wsrow[:], PB[0:1, 0:128]),
           waits=[("te", te_ws)], reads=("PB",), writes=("wsrow",))
    R.emit("vector", lambda v: v.reduce_sum(ws11[:, 0:1], wsrow[:], axis=mybir.AxisListType.X),
           reads=("wsrow",), writes=("ws11",))
    R.emit("vector", lambda v: v.tensor_scalar(ws11[:, 1:2], ws11[:, 0:1], 1.0 / (D * D), None,
                                               op0=ALU.mult), reads=("ws11",), writes=("ws11",))
    R.emit("vector", lambda v: v.tensor_scalar_max(ws11[:, 1:2], ws11[:, 1:2], 1e-6), inc="vv",
           reads=("ws11",), writes=("ws11",))
    vv_wsm = R.val("vv")
    R.emit("scalar", lambda s: s.activation(ws11[:, 3:4], ws11[:, 1:2], AF.Ln),
           waits=[("vv", vv_wsm)])
    R.emit("scalar", lambda s: s.activation(ws11[:, 2:3], ws11[:, 3:4], AF.Exp, scale=cvals[0:1, 2:3]),
           inc="ss")
    ss_wsc = R.val("ss")
    w_wsh = R.dma("scalar", lambda s: s.dma_start(out=ws_hbm[:, 0:2], in_=ws11[:, 1:3]),
                  waits=[("ss", ss_wsc)], stream="wsw", K=1)
    w_wsr = R.dma("gpsimd", lambda g: g.dma_start(out=wscol2[:], in_=ws_hbm[0, 0:2].partition_broadcast(128)),
                  waits=[w_wsh], stream="wsr", K=1)

    wq_vv_end = {}
    wwq_ret = {}
    for j in range(NWT):
        b = wf[j % 2]
        wwf = R.dma("gpsimd", lambda g, b=b, j=j: g.dma_start(out=b[:], in_=w_ffnT[j * 128:(j + 1) * 128, :]),
                    waits=[("vv", wq_vv_end.get(j - 2, 0))] +
                          ([("ss", ws_ss_end[NWT - 1])] if j < 2 else []),
                    stream="wf")
        R.emit("vector", lambda v, b=b: v.tensor_scalar(wqt[:], b[:], wscol2[:, 1:2], MAGIC,
                                                        op0=ALU.mult, op1=ALU.add),
               waits=[wwf, w_wsr, ("te", R.val("te"))] +
                     ((R.fence("wxq") + [w_cum]) if j == 0 else []),
               reads=(f"wf{j % 2}", "wscol2"), writes=("qt",))
        R.emit("vector", lambda v: v.tensor_scalar(wqt[:], wqt[:], MAGIC, -1.0,
                                                   op0=ALU.subtract, op1=ALU.max),
               reads=("qt",), writes=("qt",))
        R.emit("vector", lambda v: v.tensor_scalar(wqb[:], wqt[:], 1.0, wscol2[:, 0:1],
                                                   op0=ALU.min, op1=ALU.mult),
               inc="vv", reads=("qt", "wscol2"), writes=("xqb",),
               waits=[wwq_ret[j - 1]] if j > 0 else [])
        vv_wq = R.val("vv")
        wq_vv_end[j] = vv_wq
        wwq_ret[j] = R.dma("scalar", lambda s, j=j: s.dma_start(
            out=wq_hbm[j * 128:(j + 1) * 128, :], in_=wqb[:]),
            waits=[("vv", vv_wq)], stream="wwq")
    F_wq = R.fence("wwq")

    if cut("w"):
        return finale()

    for bi in range(NCB):
        emit_inproj("xbc", 1 + bi, bi, D + bi * 128, 128, 128)
    F_cv = R.fence("wcv")

    if cut("p1x"):
        return finale()

    R.emit("vector", lambda v: v.memset(s_run[:], 0.0), inc="vv", reads=(), writes=("s_run",))

    xt_te = {}
    xt_vv = {}

    def load_xt(ch):
        xb = XT[ch % 2]
        return R.dma("sync", lambda sy, xb=xb, ch=ch: sy.dma_start_transpose(
            xb[:], xconv_hbm[0:CONV, ch * 128:(ch + 1) * 128]),
            waits=R.fence("wcv") +
                  [("te", xt_te[ch - 2]), ("vv", xt_vv[ch - 2])] if ch - 2 in xt_te
                  else R.fence("wcv"),
            stream="lxt")

    def emit_states(ch, w_xt):
        xb = XT[ch % 2]
        R.emit("vector", lambda v, xb=xb, ch=ch: v.tensor_tensor(
            bsall[:].rearrange("p (h n) -> p h n", h=NH),
            xb[:, D:D + 64].rearrange("p (o n) -> p o n", o=1).to_broadcast([128, NH, 64]),
            scolT[:, ch * NH:(ch + 1) * NH].rearrange("p (h o) -> p h o", o=1).to_broadcast([128, NH, 64]),
            op=ALU.mult),
            waits=[w_xt, ("ss", ss_pre), ("te", R.val("te"))], inc="vv",
            reads=(f"XT{ch % 2}", "scolT"), writes=("bsall",))
        vv_bs = R.val("vv")
        for h in range(NH):
            first = (h % 8 == 0)
            last = (h % 8 == 7) or h == NH - 1
            R.emit("tensor", lambda t, h=h, xb=xb, first=first, last=last:
                   t.matmul(PB[0:64, h * 64:(h + 1) * 64], bsall[:, h * 64:(h + 1) * 64],
                            xb[:, h * 64:(h + 1) * 64], start=first, stop=last),
                   waits=[("vv", vv_bs)] if h == 0 else (), inc="te" if h == NH - 1 else None)
        te_ds = R.val("te")
        R.emit("vector", lambda v, ch=ch: v.tensor_tensor(
            s_run[:].rearrange("p (h n) -> p h n", h=NH),
            s_run[:].rearrange("p (h n) -> p h n", h=NH),
            eQa[:, ch * NH:(ch + 1) * NH].rearrange("p (h o) -> p h o", o=1).to_broadcast([64, NH, 64]),
            op=ALU.mult),
            waits=[("te", te_ds)], reads=("s_run", "eQa"), writes=("s_run",))
        R.emit("vector", lambda v: v.tensor_add(s_run[:], s_run[:], PB[0:64, 0:NH * 64]), inc="vv",
               reads=("s_run", "PB"), writes=("s_run",))

    wxts = {}
    for ch in range(min(2, NCH)):
        wxts[ch] = load_xt(ch)
    for ch in range(NCH):
        emit_states(ch, wxts[ch])
        xt_te[ch] = R.val("te")
        xt_vv[ch] = R.val("vv")
        if ch + 2 < NCH:
            wxts[ch + 2] = load_xt(ch + 2)
    vv_sfin = R.val("vv")

    if cut("p2a"):
        return finale()

    w_cci = R.dma("gpsimd", lambda g: g.dma_start(out=cc_in[:, :], in_=s_run[:]),
                  waits=[("vv", vv_sfin)], stream="cci", K=1)
    R.emit("gpsimd", lambda g: g.collective_compute(
        "AllGather", ALU.bypass, replica_groups=[list(range(n_cores))],
        ins=[cc_in[:]], outs=[cc_out[:]]),
        waits=[w_cci], inc="cc")

    if cut("ag"):
        return finale()

    for bi in range(NZB):
        emit_inproj("z", 1 + NCB + bi, bi, bi * 128, 128, 128)
    F_z = R.fence("wz")

    if cut("p1z"):
        return finale()

    slot_vv = {}
    for r in range(n_cores):
        wsl = R.dma("gpsimd", lambda g, r=r: g.dma_start(
            out=sslot[r % 2][:], in_=cc_out[r * 64:(r + 1) * 64, :]),
            waits=[("cc", 1), ("vv", slot_vv.get(r - 2, 0))] +
                  (([w_cum] + R.fence("wxq")) if r == 0 else []),
            stream="lsl")
        if r == 0:
            R.emit("vector", lambda v, r=r: v.tensor_scalar(
                s_run[:], sslot[r % 2][:], sel[:, r:r + 1], None, op0=ALU.mult),
                waits=[wsl], inc="vv", reads=(f"slots{r % 2}", "sel"), writes=("s_run",))
        else:
            R.emit("vector", lambda v, r=r: v.scalar_tensor_tensor(
                s_run[:], sslot[r % 2][:], sel[:, r:r + 1], s_run[:],
                op0=ALU.mult, op1=ALU.add),
                waits=[wsl], inc="vv",
                reads=(f"slots{r % 2}", "sel", "s_run"), writes=("s_run",))
        slot_vv[r] = R.val("vv")

    if cut("recv"):
        return finale()

    def p2m_chunk(ch, w_xt):
        xb = XT[ch % 2]
        bb = BC2[ch % 2]
        w_bc = R.dma("gpsimd", lambda g, bb=bb, ch=ch: g.dma_start(
            out=bb[:].rearrange("p (gg w) -> p gg w", gg=2),
            in_=xconv_hbm[D:D + 128, ch * 128:(ch + 1) * 128].rearrange("(gg p) w -> p gg w", gg=2)),
            waits=R.fence("wcv") + [("te", R.val("te"))], stream="lbc")
        w_cr = []
        for half in range(2):
            wpre = R.dma("gpsimd", lambda g, half=half: g.dma_start(
                out=cumrep[half][:, 0:NHH * 128].rearrange("p (h w) -> p h w", w=128),
                in_=causneg_in.unsqueeze(1).to_broadcast([128, NHH, 128])),
                waits=[("ss", R.val("ss")), ("te", R.val("te")), w_cum] + (F_z if ch == 0 else []),
                stream="lcr")
            w_acc = R.dma("gpsimd", lambda g, ch=ch, half=half: g.dma_start(
                out=cumrep[half][:, 0:NHH * 128].rearrange("p (h w) -> p h w", w=128),
                in_=cum_hbm[half * NHH:(half + 1) * NHH, ch * 128:(ch + 1) * 128]
                    .partition_broadcast(128),
                accum_op=ALU.add),
                waits=[wpre], stream="lcr")
            w_cr.append(w_acc)
        w_sz = R.dma("gpsimd", lambda g, ch=ch: g.dma_start(
            out=szc[:].rearrange("p (cc w) -> p cc w", w=128),
            in_=siluz_hbm[:, ch * 128:(ch + 1) * 128].rearrange("(cc p) w -> p cc w", p=128)),
            waits=F_z + [("vv", R.val("vv"))], stream="lsz")

        R.emit("vector", lambda v: v.tensor_copy(s_bf[:], s_run[:]),
               waits=[("te", R.val("te"))], inc="vv", reads=("s_run",), writes=("s_bf",))
        vv_sbf = R.val("vv")

        R.emit("tensor", lambda t, bb=bb: t.matmul(PB[:, 0:128], bb[:, 0:128], bb[:, 128:256],
                                                   start=True, stop=True),
               waits=[w_bc, ("vv", vv_sbf)], inc="te")
        te_g = R.val("te")
        R.emit("vector", lambda v: v.tensor_mul(gmask[:], PB[:, 0:128], caus[:]),
               waits=[("te", te_g)], inc="vv", reads=("PB", "caus"), writes=("gmask",))
        vv_gm = R.val("vv")

        for h in range(NH):
            half, hh = divmod(h, NHH)
            R.emit("scalar", lambda s, h=h, half=half, hh=hh, ch=ch: s.activation(
                dall[:, h * 128:(h + 1) * 128], cumrep[half][:, hh * 128:(hh + 1) * 128],
                AF.Exp, bias=TTa[:, ch * NH + h:ch * NH + h + 1], scale=1.0),
                waits=(w_cr + [("vv", vv_sbf)] + (F_z if ch == 0 else [])) if h == 0 else (),
                inc="ss" if h == NH - 1 else None)
        ss_d = R.val("ss")
        R.emit("vector", lambda v: v.tensor_tensor(
            mall[:].rearrange("p (h w) -> p h w", w=128),
            dall[:].rearrange("p (h w) -> p h w", w=128),
            gmask[:].rearrange("p (o w) -> p o w", o=1).to_broadcast([128, NH, 128]),
            op=ALU.mult),
            waits=[("ss", ss_d)] + (F_z if ch == 0 else []), inc="vv",
            reads=("dall", "gmask"), writes=("mall",))
        vv_m = R.val("vv")

        for h in range(NH):
            first = (h % 8 == 0)
            last = (h % 8 == 7) or h == NH - 1
            R.emit("tensor", lambda t, h=h, xb=xb, first=first, last=last:
                   t.matmul(PA[:, h * 64:(h + 1) * 64], mall[:, h * 128:(h + 1) * 128],
                            xb[:, h * 64:(h + 1) * 64], start=first, stop=last),
                   waits=[("vv", vv_m), w_xt] if h == 0 else (),
                   inc="te" if h == NH - 1 else None)
        te_intra = R.val("te")

        for pr in range(NPAIR):
            first = (pr % 4 == 0)
            last = (pr % 4 == 3) or pr == NPAIR - 1
            R.emit("tensor", lambda t, pr=pr, bb=bb, first=first, last=last:
                   t.matmul(PB[:, pr * 128:(pr + 1) * 128], bb[:, 128:256],
                            s_bf[:, pr * 128:(pr + 1) * 128], start=first, stop=last),
                   waits=[("vv", vv_gm)] if pr == 0 else (), inc="te" if pr == NPAIR - 1 else None)
        te_inter = R.val("te")

        R.emit("vector", lambda v, xb=xb: v.tensor_mul(xd[:], xb[:, 0:D], drep_bf[:]),
               waits=[w_xt], reads=(f"XT{ch % 2}", "drep_bf"), writes=("xd",))
        R.emit("vector", lambda v, ch=ch: v.tensor_tensor(
            t1[:, 0:D].rearrange("p (h n) -> p h n", h=NH),
            PB[:, 0:D].rearrange("p (h n) -> p h n", h=NH),
            ecT[:, ch * NH:(ch + 1) * NH].rearrange("p (h o) -> p h o", o=1).to_broadcast([128, NH, 64]),
            op=ALU.mult),
            waits=[("te", te_inter)], inc="vv", reads=("PB", "ecT"), writes=("t1",))
        vv_t1 = R.val("vv")
        R.emit("vector", lambda v: v.tensor_add(t2[:], t1[:, 0:D], xd[:]),
               reads=("t1", "xd"), writes=("t2",))
        R.emit("vector", lambda v: v.scalar_tensor_tensor(y0[:], PA[:, 0:D], 1.0, t2[:],
                                                          op0=ALU.mult, op1=ALU.add),
               waits=[("te", te_intra)] + R.fence("wyt"), inc="vv",
               reads=("PA", "t2"), writes=("y0",))
        vv_y0 = R.val("vv")

        R.emit("vector", lambda v, xb=xb, ch=ch: v.tensor_tensor(
            bsall[:].rearrange("p (h n) -> p h n", h=NH),
            xb[:, D:D + 64].rearrange("p (o n) -> p o n", o=1).to_broadcast([128, NH, 64]),
            scolT[:, ch * NH:(ch + 1) * NH].rearrange("p (h o) -> p h o", o=1).to_broadcast([128, NH, 64]),
            op=ALU.mult), inc="vv", reads=(f"XT{ch % 2}", "scolT"), writes=("bsall",))
        vv_bs = R.val("vv")
        for h in range(NH):
            first = (h % 8 == 0)
            last = (h % 8 == 7) or h == NH - 1
            R.emit("tensor", lambda t, h=h, xb=xb, first=first, last=last:
                   t.matmul(PB[0:64, h * 64:(h + 1) * 64], bsall[:, h * 64:(h + 1) * 64],
                            xb[:, h * 64:(h + 1) * 64], start=first, stop=last),
                   waits=[("vv", vv_bs), ("vv", vv_t1)] if h == 0 else (),
                   inc="te" if h == NH - 1 else None)
        te_ds = R.val("te")

        w_yt = []
        for cb in range(NCT):
            w_yt.append(R.dma("sync", lambda sy, cb=cb, ch=ch: sy.dma_start_transpose(
                ych[:, cb * T + ch * 128: cb * T + (ch + 1) * 128],
                y0[:, cb * 128:(cb + 1) * 128]),
                waits=[("vv", vv_y0)], stream="wyt"))
        for cb in range(NCT):
            R.emit("vector", lambda v, cb=cb, ch=ch: v.tensor_mul(
                ych[:, cb * T + ch * 128: cb * T + (ch + 1) * 128],
                ych[:, cb * T + ch * 128: cb * T + (ch + 1) * 128],
                szc[:, cb * 128:(cb + 1) * 128]),
                waits=([w_sz] + w_yt) if cb == 0 else (),
                inc="vv" if cb == NCT - 1 else None,
                reads=(f"ych{cb}", "szc"), writes=(f"ych{cb}",))

        R.emit("vector", lambda v, ch=ch: v.tensor_tensor(
            s_run[:].rearrange("p (h n) -> p h n", h=NH),
            s_run[:].rearrange("p (h n) -> p h n", h=NH),
            eQa[:, ch * NH:(ch + 1) * NH].rearrange("p (h o) -> p h o", o=1).to_broadcast([64, NH, 64]),
            op=ALU.mult),
            waits=[("te", te_ds)], reads=("s_run", "eQa"), writes=("s_run",))
        R.emit("vector", lambda v: v.tensor_add(s_run[:], s_run[:], PB[0:64, 0:NH * 64]), inc="vv",
               reads=("s_run", "PB"), writes=("s_run",))

    xt_te.clear()
    xt_vv.clear()
    xt_te[-2] = xt_te[-1] = R.val("te")
    xt_vv[-2] = xt_vv[-1] = R.val("vv")
    wxts = {}
    for ch in range(min(2, NCH)):
        wxts[ch] = load_xt(ch)
    for ch in range(NCH):
        p2m_chunk(ch, wxts[ch])
        xt_te[ch] = R.val("te")
        xt_vv[ch] = R.val("vv")
        if ch + 2 < NCH:
            wxts[ch + 2] = load_xt(ch + 2)
    vv_p2 = R.val("vv")
    F_yt = R.fence("wyt")

    if cut("p2m"):
        return finale()

    for cb in range(NCT):
        R.dma("sync", lambda sy, cb=cb: sy.dma_start_transpose(
            xq_ch[:, cb * T:(cb + 1) * T], xq_hbm[0:T, cb * 128:(cb + 1) * 128]),
            waits=R.fence("wxq") + [("vv", vv_p2)] + F_yt, stream="lxq")
    F_xq = R.fence("lxq")

    p3_vv_end = {}
    p3_wout = {}
    p3_te_end = {}
    te_p2 = R.val("te")
    for osi in range(c["NOS"]):
        wo = wos[osi % 2]
        wq = wqs[osi % 2]
        w_wo = R.dma("gpsimd", lambda g, wo=wo, osi=osi: g.dma_start(
            out=wo[:, 0:NCT * OSW].rearrange("p (cc w) -> p cc w", w=OSW),
            in_=w_outT[:, osi * OSW:(osi + 1) * OSW].rearrange("(cc p) w -> p cc w", p=128)),
            waits=[("te", p3_te_end.get(osi - 2, te_p2)), ("vv", vv_p2)], stream="lwo")
        w_wq = R.dma("gpsimd", lambda g, wq=wq, osi=osi: g.dma_start(
            out=wq[:, 0:NCT * OSW].rearrange("p (cc w) -> p cc w", w=OSW),
            in_=wq_hbm[:, osi * OSW:(osi + 1) * OSW].rearrange("(cc p) w -> p cc w", p=128)),
            waits=F_wq, stream="lwo")
        for ti in range(NT):
            it = osi * NT + ti
            ubuf = uts[it % 4]
            obuf = outs[it % 4]
            w_u = R.dma("gpsimd", lambda g, ubuf=ubuf, ti=ti, osi=osi: g.dma_start(
                out=ubuf[:], in_=u_loc[ti * 128:(ti + 1) * 128, osi * OSW:(osi + 1) * OSW]),
                waits=[("vv", p3_vv_end.get(it - 4, vv_p2))], stream="lu3", K=4)
            pv = PA[:, (it % 4) * 512:(it % 4) * 512 + OSW]
            for ci in range(NCT):
                wl = ([w_wo, w_wq, ("vv", p3_vv_end.get(it - 4, vv_p2))] + F_xq) \
                    if ci == 0 else []
                R.emit("tensor", lambda t, pv=pv, ci=ci, ti=ti, wo=wo:
                       t.matmul(pv, ych[:, ci * T + ti * 128:ci * T + (ti + 1) * 128],
                                wo[:, ci * OSW:(ci + 1) * OSW], start=(ci == 0), stop=False),
                       waits=wl)
            for ci in range(NCT):
                R.emit("tensor", lambda t, pv=pv, ci=ci, ti=ti, wq=wq:
                       t.matmul(pv, xq_ch[:, ci * T + ti * 128:ci * T + (ti + 1) * 128],
                                wq[:, ci * OSW:(ci + 1) * OSW], start=False, stop=(ci == NCT - 1)),
                       inc="te" if ci == NCT - 1 else None)
            te_t = R.val("te")
            R.emit("vector", lambda v, pv=pv, ubuf=ubuf, obuf=obuf: v.scalar_tensor_tensor(
                obuf[:], pv, 1.0, ubuf[:], op0=ALU.mult, op1=ALU.add),
                waits=[("te", te_t), w_u] +
                      ([p3_wout[it - 4]] if it - 4 in p3_wout else []), inc="vv",
                reads=("PA", f"uts{it % 4}"), writes=(f"outs{it % 4}",))
            p3_vv_end[it] = R.val("vv")
            p3_wout[it] = R.dma("scalar", lambda s, obuf=obuf, ti=ti, osi=osi: s.dma_start(
                out=out_loc[ti * 128:(ti + 1) * 128, osi * OSW:(osi + 1) * OSW], in_=obuf[:]),
                waits=[("vv", p3_vv_end[it])], stream="wout", K=4)
        p3_te_end[osi] = R.val("te")

    return finale()


def make_core_inputs(c, u, conv_w, conv_b, A_log, D_param, dt_bias):
    D, T, NH, NCB = c["D"], c["T"], c["NH"], c["NCB"]
    n_cores = c["n_cores"]
    cw = np.ascontiguousarray(conv_w.reshape(NCB, 128, 4).transpose(1, 0, 2).reshape(128, NCB * 4))
    cb = np.ascontiguousarray(conv_b.reshape(NCB, 128).T)
    dtb = np.ascontiguousarray(dt_bias[:, None])
    alg = np.ascontiguousarray(A_log[:, None])
    drep = np.ascontiguousarray(np.repeat(D_param, 64)[None, :])
    caus = np.triu(np.ones((128, 128), np.float32))
    causneg = np.where(caus > 0, 0.0, -60000.0).astype(np.float32)
    eye = np.eye(128, dtype=np.float32)
    ones1 = np.ones((1, 128), np.float32)
    onecol = np.ones((128, 1), np.float32)
    zer32 = np.zeros((NH, 128), np.float32)
    zpad = np.zeros((16, D), BF)
    cvals = np.zeros((128, 8), np.float32)
    cvals[:, 0] = 1e-5
    cvals[:, 1] = -0.5
    cvals[:, 2] = -1.0
    cvals[:, 3] = 1.0
    maps = []
    for k in range(n_cores):
        b, half = k // 2, k % 2
        u_b = u[b]
        u_slice = np.ascontiguousarray(u_b[half * T:(half + 1) * T])
        halo = np.zeros((128, D), np.float32)
        if half == 1:
            halo[0:3] = u_b[T - 3:T]
        selv = np.zeros((64, n_cores), np.float32)
        if half == 1:
            selv[:, k - 1] = 1.0
        maps.append(dict(
            u_loc=u_slice, u_halo=halo,
            conv_w_l=cw, conv_b_l=cb, dt_bias_c=dtb, a_log_c=alg, d_rep=drep,
            sel_in=selv, caus_in=caus, causneg_in=causneg, eye_in=eye, ones_in=ones1,
            onecol_in=onecol, zer32_in=zer32, zpad_in=zpad, cvals_in=cvals,
        ))
    return maps


_NC_CACHE = {}


def kernel(L0_attn_u, L0_attn_W_in, L0_attn_conv_w, L0_attn_conv_b, L0_attn_W_out,
           L0_attn_A_log, L0_attn_D_param, L0_attn_dt_bias, L0_ffn_weight):
    c = mkcfg(D=2048, T=2048, n_cores=8)
    u = np.ascontiguousarray(np.asarray(L0_attn_u, dtype=np.float32))
    maps = make_core_inputs(
        c, u,
        np.ascontiguousarray(np.asarray(L0_attn_conv_w, np.float32).reshape(c["CONV"], 4)),
        np.ascontiguousarray(np.asarray(L0_attn_conv_b, np.float32)),
        np.ascontiguousarray(np.asarray(L0_attn_A_log, np.float32)),
        np.ascontiguousarray(np.asarray(L0_attn_D_param, np.float32)),
        np.ascontiguousarray(np.asarray(L0_attn_dt_bias, np.float32)),
    )
    w_inT = np.ascontiguousarray(np.asarray(L0_attn_W_in, np.float32).T)
    w_outT = np.ascontiguousarray(np.asarray(L0_attn_W_out, np.float32).T)
    w_ffnT = np.ascontiguousarray(np.asarray(L0_ffn_weight, np.float32).T)
    for m in maps:
        m["w_inT"] = w_inT
        m["w_outT"] = w_outT
        m["w_ffnT"] = w_ffnT
    key = (c["D"], c["T"], c["n_cores"])
    if key not in _NC_CACHE:
        _NC_CACHE[key] = build_nc(c)
    nc = _NC_CACHE[key]
    res = run_bass_kernel_spmd(nc, maps, core_ids=list(range(c["n_cores"])))
    B, L = u.shape[0], u.shape[1]
    out = np.empty((B, L, c["D"]), np.float32)
    for k in range(c["n_cores"]):
        b, half = k // 2, k % 2
        out[b, half * c["T"]:(half + 1) * c["T"], :] = res.results[k]["out_loc"]
    return out


# version 04 (final)
# speedup vs baseline: 1.3169x; 1.1914x over previous
import contextlib
import numpy as np
import ml_dtypes
import concourse.bass as bass
import concourse.mybir as mybir
from concourse.bass_utils import run_bass_kernel_spmd

F32 = mybir.dt.float32
BF16 = mybir.dt.bfloat16
I32 = mybir.dt.int32
AF = mybir.ActivationFunctionType
ALU = mybir.AluOpType
BF = ml_dtypes.bfloat16

MAGIC = float(3 * 2.0 ** 22)
PIN = 384.5
PINI = 384.0
ANDMASK = 0xFFFF8000 - (1 << 32)


def mkcfg(D, T, n_cores):
    NH = D // 64
    c = dict(
        D=D, T=T, n_cores=n_cores, NH=NH, NPAIR=NH // 2,
        CONV=D + 128, NCB=D // 128 + 1, NZB=D // 128,
        DIP=2 * D + 128 + NH, NCH=T // 128, NT=T // 128,
        TH=T + 3,
    )
    c["XNS"] = ((c["TH"] + 15) // 16) * 16
    strips = []
    off = 0
    while off < c["TH"]:
        w = min(512, c["TH"] - off)
        strips.append((off, w))
        off += w
    c["XBC_STRIPS"] = strips
    zstrips = []
    off = 3
    while off < c["TH"]:
        w = min(512, c["TH"] - off)
        zstrips.append((off, w))
        off += w
    c["Z_STRIPS"] = zstrips
    c["OSW"] = min(256, D)
    c["NOS"] = D // c["OSW"]
    return c


class Rec:

    ENGINES = ("gpsimd", "sync", "scalar", "vector", "tensor")

    def __init__(self):
        self.ops = {e: [] for e in self.ENGINES}
        self.ctr = {s: 0 for s in ("te", "vv", "ss", "cc")}
        self.streams = {}
        self._vw = set()
        self._spacer_fn = None
        self.n_spacers = 0

    def _sem(self, name):
        if name not in self.ctr:
            self.ctr[name] = 0
        return name

    def emit(self, eng, fn, waits=(), inc=None, inc_amt=1, reads=None, writes=None):
        if eng == "vector":
            rd = {"*"} if reads is None else set(reads)
            wr = {"*"} if writes is None else set(writes)
            prev = self._vw
            if prev and rd and ("*" in prev or "*" in rd or (rd & prev)):
                assert self._spacer_fn is not None
                self.ops["vector"].append((self._spacer_fn, [], None))
                self.n_spacers += 1
            self._vw = wr
        w = []
        for s, v in waits:
            if v > 0:
                w.append((s, v))
        i = None
        if inc is not None:
            self._sem(inc)
            self.ctr[inc] += inc_amt
            i = (inc, inc_amt)
        self.ops[eng].append((fn, w, i))

    def dma(self, eng, fn, waits=(), stream="misc", K=2):
        st = self.streams.setdefault(stream, {"n": 0, "K": K})
        assert st["K"] == K, f"stream {stream} K mismatch"
        n = st["n"]
        slot = n % K
        sem = self._sem(f"d_{stream}_{slot}")
        prior = 16 * (n // K)
        self.emit(eng, fn, list(waits) + [(sem, prior)], inc=sem, inc_amt=16)
        st["n"] = n + 1
        return (sem, 16 * (n // K + 1))

    def fence(self, *streams):
        out = []
        names = streams if streams else list(self.streams.keys())
        for nm in names:
            st = self.streams.get(nm)
            if not st:
                continue
            K = st["K"]
            for slot in range(K):
                cnt = (st["n"] + K - 1 - slot) // K
                if cnt > 0:
                    out.append((f"d_{nm}_{slot}", 16 * cnt))
        return out

    def val(self, sem):
        return self.ctr[sem]

    def replay(self, eng_name, eng, sems):
        last = {}
        for fn, waits, inc in self.ops[eng_name]:
            for s, v in waits:
                if last.get(s, 0) < v:
                    eng.wait_ge(sems[s], v)
                    last[s] = v
            inst = fn(eng)
            if inc is not None:
                s, amt = inc
                inst.then_inc(sems[s], amt)


def _finish(nc, es, R):
    sems = {name: es.enter_context(nc.semaphore(name)) for name in R.ctr}
    block = es.enter_context(nc.Block())

    @block.gpsimd
    def _(g):
        R.replay("gpsimd", g, sems)

    @block.sync
    def _(sy):
        R.replay("sync", sy, sems)

    @block.scalar
    def _(s):
        R.replay("scalar", s, sems)

    @block.vector
    def _(v):
        R.replay("vector", v, sems)

    @block.tensor
    def _(t):
        R.replay("tensor", t, sems)

    es.close()
    return nc


def build_nc(c, detect_races=False, upto=None, dbg=False):
    D, T, NH = c["D"], c["T"], c["NH"]
    NPAIR, CONV, NCB, NZB = c["NPAIR"], c["CONV"], c["NCB"], c["NZB"]
    NCH, NT, TH, XNS = c["NCH"], c["NT"], c["TH"], c["XNS"]
    n_cores = c["n_cores"]
    NCT = D // 128
    NHH = NH // 2
    OSW = c["OSW"]
    STKB = 32

    nc = bass.Bass("TRN2", target_bir_lowering=False, debug=False,
                   detect_race_conditions=detect_races)

    def din(name, shape, dt=F32):
        return nc.dram_tensor(name, shape, dt, kind="ExternalInput").ap()

    u_loc = din("u_loc", [T, D])
    u_halo = din("u_halo", [128, D])
    w_inT = din("w_inT", [D, c["DIP"]])
    w_outT = din("w_outT", [D, D])
    w_ffnT = din("w_ffnT", [D, D])
    conv_w_l = din("conv_w_l", [128, NCB * 4])
    conv_b_l = din("conv_b_l", [128, NCB])
    dt_bias_c = din("dt_bias_c", [NH, 1])
    a_log_c = din("a_log_c", [NH, 1])
    dmaj_in = din("dmaj_in", [128, NCT])
    sel_in = din("sel_in", [64, n_cores])
    caus_in = din("caus_in", [128, 128])
    eye_in = din("eye_in", [128, 128])
    ones_in = din("ones_in", [1, 128])
    onecol_in = din("onecol_in", [128, 1])
    zer32_in = din("zer32_in", [NH, 128])
    zpad_in = din("zpad_in", [16, D], BF16)
    cvals_in = din("cvals_in", [128, 8])

    out_loc = nc.dram_tensor("out_loc", [T, D], F32, kind="ExternalOutput").ap()

    kd = dict(kind="ExternalOutput") if dbg else {}
    xn_hbm = nc.dram_tensor("xn_hbm", [XNS, D], BF16, **kd).ap()
    xq_hbm = nc.dram_tensor("xq_hbm", [T, D], BF16, **kd).ap()
    xconv_hbm = nc.dram_tensor("xconv_hbm", [CONV, T], BF16, **kd).ap()
    siluz_hbm = nc.dram_tensor("siluz_hbm", [D, T], BF16, **kd).ap()
    cum_hbm = nc.dram_tensor("cum_hbm", [NH, T], F32, **kd).ap()
    ec_hbm = nc.dram_tensor("ec_hbm", [NH, T], BF16).ap()
    wq_hbm = nc.dram_tensor("wq_hbm", [D, D], BF16, **kd).ap()
    ws_hbm = nc.dram_tensor("ws_hbm", [1, 3], F32).ap()
    cc_in = nc.dram_tensor("cc_in", [64, NH * 64], F32).ap()
    cc_out = nc.dram_tensor("cc_out", [64 * n_cores, NH * 64], F32,
                            addr_space="Shared" if n_cores > 4 else "Local").ap()

    es = contextlib.ExitStack()
    _auto = []

    def sbuf(name, shape, dt=F32):
        h = es.enter_context(nc.sbuf_tensor(name, shape, dt))
        _auto.append((h, shape, dt))
        return h

    SBUF_CAP = 229344

    def tsize(shape, dt):
        n = 1
        for s in shape[1:]:
            n *= s
        return ((n * mybir.dt.size(dt) + 63) // 64) * 64

    specA, specC = {}, {}

    def chain(spec, items):
        cur = 0
        for nm, shp, dt in items:
            spec[nm] = (shp, dt, cur)
            cur += tsize(shp, dt)
        return cur

    wA0 = chain(specA, [("ub0", [128, D], F32), ("ub1", [128, D], F32),
                        ("xnf", [128, D], F32), ("xnb", [128, D], BF16),
                        ("qt", [128, D], F32), ("xqb", [128, D], BF16),
                        ("wf0", [128, D], F32), ("wf1", [128, D], F32),
                        ("qsq", [128, D], F32)])
    wA1 = chain(specA, [("sc0", [32, T], F32), ("sc1", [32, T], F32),
                        ("sc2", [32, T], F32), ("sc3", [32, T], F32),
                        ("stk", [3 * STKB, T], F32)])
    wA2 = chain(specA, [("XT0", [128, CONV], BF16), ("XT1", [128, CONV], BF16),
                        ("bsall", [128, NH * 64], BF16), ("BC2a", [64, 256], BF16),
                        ("BC2b", [64, 256], BF16), ("gmask", [128, 128], BF16),
                        ("szc", [128, D], BF16), ("y0", [128, D], BF16),
                        ("slots0", [64, NH * 64], F32), ("slots1", [64, NH * 64], F32)])
    specA["xq_ch"] = ([128, NCT * T], BF16, 0)
    widthA = max(wA0, wA1, wA2, tsize([128, NCT * T], BF16))

    widthB = max(tsize([128, NCT * XNS], BF16), tsize([128, NCT * T], BF16))

    wC0 = chain(specC, [("wstage0", [128, NCT * 128], BF16), ("wstage1", [128, NCT * 128], BF16),
                        ("stage", [128, TH], BF16), ("convacc", [128, T], F32),
                        ("xcb", [128, T], BF16), ("dt_toks", [NH, T], F32)])
    wC1 = chain(specC, [("cumrep0", [128, NHH * 128], F32), ("cumrep1", [128, NHH * 128], F32),
                        ("e_sb", [128, NCT * 128], BF16),
                        ("dall", [128, NH * 128], BF16), ("mall", [128, NH * 128], BF16),
                        ("q1", [128, D], F32), ("xcmaj", [128, NCT * 128], BF16)])
    wC2 = chain(specC, [("wos0", [128, NCT * OSW], BF16), ("wos1", [128, NCT * OSW], BF16),
                        ("wqs0", [128, NCT * OSW], BF16), ("wqs1", [128, NCT * OSW], BF16),
                        ("uts0", [128, OSW], F32), ("uts1", [128, OSW], F32),
                        ("uts2", [128, OSW], F32), ("uts3", [128, OSW], F32),
                        ("outs0", [128, OSW], F32), ("outs1", [128, OSW], F32),
                        ("outs2", [128, OSW], F32), ("outs3", [128, OSW], F32)])
    widthC = max(wC0, wC1, wC2)

    ARENA = widthA + widthB + widthC
    A0 = SBUF_CAP - ARENA - 1024
    offB, offC = widthA, widthA + widthB

    def at(spec, base, nm):
        shp, dt, off = spec[nm]
        return nc.alloc_sbuf_tensor_at(nm, shp, dt, offset=A0 + base + off).ap()

    caus = sbuf("caus", [128, 128])
    eye = sbuf("eye", [128, 128])
    ones1 = sbuf("ones1", [1, 128])
    onecol = sbuf("onecol", [128, 1])
    zer32 = sbuf("zer32", [NH, 128])
    cwsb = sbuf("cwsb", [128, NCB * 4])
    cbsb = sbuf("cbsb", [128, NCB])
    dtb = sbuf("dtb", [NH, 1])
    alog = sbuf("alog", [NH, 1])
    dmaj = sbuf("dmaj", [128, NCT])
    sel = sbuf("sel", [64, n_cores])
    cvals = sbuf("cvals", [128, 8])
    mv = sbuf("mv", [128, 2])
    lnt = sbuf("lnt", [128, 4])
    amx = sbuf("amx", [128, 4])
    wsacc = sbuf("wsacc", [128, 1])
    wscol2 = sbuf("wscol2", [128, 2])
    ws11 = sbuf("ws11", [1, 4])
    wsrow = sbuf("wsrow", [1, 128])
    TTa = sbuf("TTa", [128, NCH * NH])
    scolT = sbuf("scolT", [128, NCH * NH], BF16)
    eQa = sbuf("eQa", [64, NCH * NH])
    cqr = sbuf("cqr", [1, NH])
    s_run = sbuf("s_run", [64, NH * 64])
    s_bf = sbuf("s_bf", [64, NH * 64], BF16)
    spc = sbuf("spc", [128, 1])

    stack_top = max(nc.lookup_mloc(h).addr + tsize(shp, dt) for h, shp, dt in _auto)
    assert stack_top <= A0, f"left stack {stack_top} overlaps arena base {A0}"

    ub = [at(specA, 0, "ub0"), at(specA, 0, "ub1")]
    xnf = at(specA, 0, "xnf")
    xnb = at(specA, 0, "xnb")
    qt = at(specA, 0, "qt")
    xqb = at(specA, 0, "xqb")
    wf = [at(specA, 0, "wf0"), at(specA, 0, "wf1")]
    qsq = at(specA, 0, "qsq")
    wqt = qt
    wqb = xqb
    wabs = xnf
    sc_ = [at(specA, 0, f"sc{i}") for i in range(4)]
    stk = at(specA, 0, "stk")
    XT = [at(specA, 0, "XT0"), at(specA, 0, "XT1")]
    bsall = at(specA, 0, "bsall")
    BC2 = [at(specA, 0, "BC2a"), at(specA, 0, "BC2b")]
    gmask = at(specA, 0, "gmask")
    szc = at(specA, 0, "szc")
    y0 = at(specA, 0, "y0")
    sslot = [at(specA, 0, "slots0"), at(specA, 0, "slots1")]
    xq_ch = at(specA, 0, "xq_ch")
    xn_ch = nc.alloc_sbuf_tensor_at("xn_ch", [128, NCT * XNS], BF16, offset=A0 + offB).ap()
    ych = nc.alloc_sbuf_tensor_at("ych", [128, NCT * T], BF16, offset=A0 + offB).ap()
    wstage = [at(specC, offC, "wstage0"), at(specC, offC, "wstage1")]
    stage = at(specC, offC, "stage")
    convacc = at(specC, offC, "convacc")
    xcb = at(specC, offC, "xcb")
    dt_toks = at(specC, offC, "dt_toks")
    cumrep = [at(specC, offC, "cumrep0"), at(specC, offC, "cumrep1")]
    e_sb = at(specC, offC, "e_sb")
    dall = at(specC, offC, "dall")
    mall = at(specC, offC, "mall")
    q1 = at(specC, offC, "q1")
    xcmaj = at(specC, offC, "xcmaj")
    wos = [at(specC, offC, "wos0"), at(specC, offC, "wos1")]
    wqs = [at(specC, offC, "wqs0"), at(specC, offC, "wqs1")]
    uts = [at(specC, offC, f"uts{i}") for i in range(4)]
    outs = [at(specC, offC, f"outs{i}") for i in range(4)]

    PA = es.enter_context(nc.psum_tensor("PA", [128, 2048], F32))
    PB = es.enter_context(nc.psum_tensor("PB", [128, 2048], F32))

    R = Rec()
    R._spacer_fn = lambda v: v.tensor_copy(spc[:], onecol[:])

    def cut(name):
        return upto is not None and upto == name

    def finale():
        R.emit("vector", lambda v: v.engine_nop(),
               waits=R.fence() + [("te", R.val("te")), ("ss", R.val("ss"))],
               reads=(), writes=())
        return _finish(nc, es, R)

    for dst, src in ((caus, caus_in), (eye, eye_in), (ones1, ones_in),
                     (onecol, onecol_in), (zer32, zer32_in), (cwsb, conv_w_l),
                     (cbsb, conv_b_l), (dtb, dt_bias_c), (alog, a_log_c),
                     (sel, sel_in), (cvals, cvals_in), (dmaj, dmaj_in)):
        R.dma("gpsimd", lambda g, dst=dst, src=src: g.dma_start(out=dst[:], in_=src), stream="cst")
    if XNS > TH:
        R.dma("gpsimd", lambda g: g.dma_start(out=xn_hbm[TH:XNS, :], in_=zpad_in[0:XNS - TH, :]),
              stream="cst")
    F_cst = R.fence("cst")
    for e in Rec.ENGINES:
        R.emit(e, lambda eng: None, waits=F_cst)

    p0_vv_end = {}
    p0_ss_end = {}
    p0_vv_mv = {}

    def p0_tile(i, halo):
        buf = ub[(i + 1) % 2 if halo else i % 2]
        src = u_halo if halo else u_loc[i * 128:(i + 1) * 128, :]
        key = -1 if halo else i
        wu = R.dma("gpsimd", lambda g, buf=buf, src=src: g.dma_start(out=buf[:], in_=src),
                   waits=[("vv", p0_vv_end.get(key - 2, 0)), ("ss", p0_ss_end.get(key - 2, 0))],
                   stream="u0")
        w0 = [wu, ("ss", R.val("ss"))] + R.fence("wxn", "wxq")
        bnm = f"ub{(i + 1) % 2 if halo else i % 2}"
        R.emit("scalar", lambda s, buf=buf: s.activation(qsq[:], buf[:], AF.Square),
               waits=[wu, ("vv", p0_vv_mv.get(key - 1, 0))], inc="ss")
        ss_sq = R.val("ss")
        p0_ss_end[key] = ss_sq
        R.emit("vector", lambda v, buf=buf: v.reduce_sum(mv[:, 0:1], buf[:], axis=mybir.AxisListType.X),
               waits=w0, reads=(bnm,), writes=("mv",))
        R.emit("vector", lambda v: v.reduce_sum(mv[:, 1:2], qsq[:], axis=mybir.AxisListType.X),
               waits=[("ss", ss_sq)], reads=("qsq",), writes=("mv",))
        R.emit("vector", lambda v: v.tensor_scalar(mv[:, 0:1], mv[:, 0:1], 1.0 / D, None, op0=ALU.mult),
               reads=("mv",), writes=("mv",))
        R.emit("vector", lambda v: v.tensor_scalar(mv[:, 1:2], mv[:, 1:2], 1.0 / D, None, op0=ALU.mult),
               reads=("mv",), writes=("mv",))
        R.emit("vector", lambda v: v.tensor_mul(lnt[:, 3:4], mv[:, 0:1], mv[:, 0:1]),
               reads=("mv",), writes=("lnt",))
        R.emit("vector", lambda v: v.tensor_sub(mv[:, 1:2], mv[:, 1:2], lnt[:, 3:4]), inc="vv",
               reads=("mv", "lnt"), writes=("mv",))
        vv_mv = R.val("vv")
        p0_vv_mv[key] = vv_mv
        R.emit("scalar", lambda s: s.activation(lnt[:, 0:1], mv[:, 1:2], AF.Ln, bias=cvals[:, 0:1],
                                                scale=1.0),
               waits=[("vv", vv_mv)])
        R.emit("scalar", lambda s: s.activation(lnt[:, 1:2], lnt[:, 0:1], AF.Exp, scale=cvals[:, 1:2]),
               inc="ss")
        ss_r = R.val("ss")
        R.emit("vector", lambda v, buf=buf: v.tensor_scalar(xnf[:], buf[:], mv[:, 0:1], lnt[:, 1:2],
                                                            op0=ALU.subtract, op1=ALU.mult),
               waits=[("ss", ss_r)], inc="vv", reads=(bnm, "mv", "lnt"), writes=("xnf",))
        vv_xnb = R.val("vv")
        R.dma("gpsimd", lambda g, i=i, halo=halo: g.dma_start(
            out=xn_hbm[0:3, :] if halo else xn_hbm[3 + i * 128: 3 + (i + 1) * 128, :],
            in_=xnf[0:3, :] if halo else xnf[:]),
            waits=[("vv", vv_xnb)], stream="wxn")
        if not halo:
            R.emit("vector", lambda v: v.reduce_max(amx[:, 0:1], xnf[:], axis=mybir.AxisListType.X,
                                                    apply_absolute_value=True),
                   reads=("xnf",), writes=("amx",))
            R.emit("vector", lambda v: v.tensor_scalar_max(amx[:, 1:2], amx[:, 0:1], 1e-8), inc="vv",
                   reads=("amx",), writes=("amx",))
            vv_am = R.val("vv")
            R.emit("scalar", lambda s: s.activation(lnt[:, 2:3], amx[:, 1:2], AF.Ln),
                   waits=[("vv", vv_am)])
            R.emit("scalar", lambda s: s.activation(amx[:, 2:3], lnt[:, 2:3], AF.Exp, scale=cvals[:, 2:3]),
                   inc="ss")
            ss_am = R.val("ss")
            R.emit("vector", lambda v: v.tensor_scalar(amx[:, 2:3], amx[:, 2:3], 127.0, None, op0=ALU.mult),
                   waits=[("ss", ss_am)], reads=("amx",), writes=("amx",))
            R.emit("vector", lambda v: v.tensor_scalar(amx[:, 3:4], amx[:, 1:2], 1.0 / 127.0, None,
                                                       op0=ALU.mult), reads=("amx",), writes=("amx",))
            R.emit("vector", lambda v: v.tensor_scalar(qt[:], xnf[:], amx[:, 2:3], MAGIC,
                                                       op0=ALU.mult, op1=ALU.add),
                   reads=("xnf", "amx"), writes=("qt",))
            R.emit("vector", lambda v: v.tensor_scalar(xqb[:], qt[:], MAGIC, amx[:, 3:4],
                                                       op0=ALU.subtract, op1=ALU.mult), inc="vv",
                   reads=("qt", "amx"), writes=("xqb",))
            vv_xq = R.val("vv")
            R.dma("scalar", lambda s, i=i: s.dma_start(out=xq_hbm[i * 128:(i + 1) * 128, :], in_=xqb[:]),
                  waits=[("vv", vv_xq)], stream="wxq")
        p0_vv_end[key] = R.val("vv")

    p0_tile(0, halo=True)
    for i in range(NT):
        p0_tile(i, halo=False)
    F_p0w = R.fence("wxn", "wxq")

    if cut("p0"):
        return finale()

    for cb in range(NCT):
        R.dma("sync", lambda sy, cb=cb: sy.dma_start_transpose(
            xn_ch[:, cb * XNS:(cb + 1) * XNS],
            xn_hbm[0:XNS, cb * 128:(cb + 1) * 128]),
            waits=F_p0w + F_cst, stream="lxn")
    F_xn = R.fence("lxn")

    if cut("xn"):
        return finale()

    def pa_strip(si, width):
        return PA[:, (si % 4) * 512:(si % 4) * 512 + width]

    p1_te_end = {}
    p1_wr_end = {}
    p1_ss_end = {}

    def emit_inproj(kind, key, bi, wcol0, wwidth, outparts):
        buf = wstage[key % 2]
        wws = R.dma("gpsimd", lambda g, buf=buf, wcol0=wcol0, wwidth=wwidth: g.dma_start(
            out=buf[:, 0:NCT * wwidth].rearrange("p (cc w) -> p cc w", w=wwidth),
            in_=w_inT[:, wcol0:wcol0 + wwidth].rearrange("(cc p) w -> p cc w", p=128)),
            waits=[("te", p1_te_end.get(key - 2, 0))], stream="wst")
        strips = c["XBC_STRIPS"] if kind == "xbc" else c["Z_STRIPS"]
        vv_before = R.val("vv")
        ss_before = R.val("ss")
        strip_evac = {}
        for si, (off, width) in enumerate(strips):
            pv = pa_strip(si, width)
            for ci in range(NCT):
                wl = []
                if ci == 0:
                    if si == 0:
                        wl = [wws, ("vv", vv_before), ("ss", ss_before)] + F_xn
                    elif si >= 4:
                        wl = [(("ss" if kind == "z" else "vv"), strip_evac[si - 4])]
                R.emit("tensor", lambda t, pv=pv, buf=buf, ci=ci, off=off, width=width,
                       wwidth=wwidth, outparts=outparts, first=(ci == 0), last=(ci == NCT - 1):
                       t.matmul(pv[0:outparts, :], buf[:, ci * wwidth:(ci + 1) * wwidth],
                                xn_ch[:, ci * XNS + off: ci * XNS + off + width],
                                start=first, stop=last),
                       waits=wl, inc="te" if ci == NCT - 1 else None)
            te_t = R.val("te")
            if kind == "xbc":
                R.emit("vector", lambda v, pv=pv, off=off, width=width:
                       v.tensor_copy(stage[:, off:off + width], pv),
                       waits=[("te", te_t)], inc="vv", reads=("PA",), writes=("stage",))
                strip_evac[si] = R.val("vv")
            elif kind == "z":
                R.emit("scalar", lambda s, pv=pv, off=off, width=width:
                       s.activation(xcb[:, off - 3:off - 3 + width], pv, AF.Silu),
                       waits=[("te", te_t), p1_wr_end.get(key - 1, ("ss", 0))], inc="ss")
                strip_evac[si] = R.val("ss")
            else:
                R.emit("vector", lambda v, pv=pv, off=off, width=width, outparts=outparts:
                       v.tensor_copy(dt_toks[:, off - 3:off - 3 + width], pv[0:outparts, :]),
                       waits=[("te", te_t)], inc="vv", reads=("PA",), writes=("dt_toks",))
        p1_te_end[key] = R.val("te")
        if kind == "xbc":
            R.emit("vector", lambda v, bi=bi:
                   v.tensor_scalar(convacc[:], stage[:, 0:T], cwsb[:, bi * 4:bi * 4 + 1], None,
                                   op0=ALU.mult),
                   waits=[("ss", p1_ss_end.get(key - 1, 0))],
                   reads=("stage", "cwsb"), writes=("convacc",))
            R.emit("vector", lambda v, bi=bi:
                   v.tensor_scalar(q1[:, 0:T], stage[:, 1:1 + T], cwsb[:, bi * 4 + 1:bi * 4 + 2], None,
                                   op0=ALU.mult),
                   reads=("stage", "cwsb"), writes=("cvt",))
            R.emit("vector", lambda v, bi=bi:
                   v.scalar_tensor_tensor(convacc[:], stage[:, 2:2 + T],
                                          cwsb[:, bi * 4 + 2:bi * 4 + 3],
                                          convacc[:], op0=ALU.mult, op1=ALU.add),
                   reads=("stage", "cwsb", "convacc"), writes=("convacc",))
            R.emit("vector", lambda v, bi=bi:
                   v.scalar_tensor_tensor(q1[:, 0:T], stage[:, 3:3 + T],
                                          cwsb[:, bi * 4 + 3:bi * 4 + 4],
                                          q1[:, 0:T], op0=ALU.mult, op1=ALU.add),
                   reads=("stage", "cwsb", "cvt"), writes=("cvt",))
            R.emit("vector", lambda v: v.tensor_add(convacc[:], convacc[:], q1[:, 0:T]), inc="vv",
                   reads=("convacc", "cvt"), writes=("convacc",))
            vv_conv = R.val("vv")
            R.emit("scalar", lambda s, bi=bi:
                   s.activation(xcb[:], convacc[:], AF.Silu, bias=cbsb[:, bi:bi + 1], scale=1.0),
                   waits=[("vv", vv_conv), p1_wr_end.get(key - 1, ("ss", 0))], inc="ss")
            p1_ss_end[key] = R.val("ss")
            p1_wr_end[key] = R.dma("scalar", lambda s, bi=bi: s.dma_start(
                out=xconv_hbm[bi * 128:(bi + 1) * 128, :], in_=xcb[:]),
                waits=[("ss", p1_ss_end[key])], stream="wcv")
        elif kind == "z":
            p1_ss_end[key] = R.val("ss")
            p1_wr_end[key] = R.dma("scalar", lambda s, bi=bi: s.dma_start(
                out=siluz_hbm[bi * 128:(bi + 1) * 128, :], in_=xcb[:]),
                waits=[("ss", p1_ss_end[key])], stream="wz")

    emit_inproj("dt", 0, 0, 2 * D + 128, NH, NH)
    vv_dt = R.val("vv")

    spm = sc_[0][0:NH, :]
    spa = sc_[1][0:NH, :]
    spe = sc_[2][0:NH, :]
    spl = sc_[3][0:NH, :]
    R.emit("vector", lambda v: v.tensor_scalar(spm, dt_toks[:], dtb[:], None, op0=ALU.add),
           waits=[("vv", vv_dt)] + F_cst + F_p0w, inc="vv",
           reads=("dt_toks", "dtb"), writes=("sc0",))
    vv_t = R.val("vv")
    R.emit("scalar", lambda s: s.activation(spa, spm, AF.Abs), waits=[("vv", vv_t)])
    R.emit("scalar", lambda s: s.activation(spe, spa, AF.Exp, scale=cvals[0:NH, 2:3]))
    R.emit("scalar", lambda s: s.activation(spl, spe, AF.Ln, bias=cvals[0:NH, 3:4]),
           waits=R.fence("wxn"), inc="ss")
    ss_t = R.val("ss")
    mx = sc_[2][0:NH, :]
    dtv = sc_[1][0:NH, :]
    logda = sc_[0][0:NH, :]
    logdt = sc_[3][0:NH, :]
    R.emit("vector", lambda v: v.tensor_scalar_max(mx, spm, 0.0), waits=[("ss", ss_t)],
           reads=("sc0",), writes=("sc2",))
    R.emit("vector", lambda v: v.tensor_add(dtv, mx, spl),
           reads=("sc2", "sc3"), writes=("sc1",))
    R.emit("scalar", lambda s: s.activation(lnt[0:NH, 3:4], alog[:], AF.Exp), inc="ss")
    ss_t = R.val("ss")
    R.emit("vector", lambda v: v.tensor_scalar(logda, dtv, lnt[0:NH, 3:4], -1.0,
                                               op0=ALU.mult, op1=ALU.mult),
           waits=[("ss", ss_t)], inc="vv", reads=("sc1", "lnt"), writes=("sc0",))
    vv_t = R.val("vv")
    R.emit("scalar", lambda s: s.activation(logdt, dtv, AF.Ln), waits=[("vv", vv_t)], inc="ss")
    ss_logdt = R.val("ss")
    if NH < STKB:
        R.emit("vector", lambda v: v.memset(stk[:], 0.0), reads=(), writes=("stk",))
    for ch in range(NCH):
        sl = slice(ch * 128, (ch + 1) * 128)
        R.emit("vector", lambda v, sl=sl: v.tensor_tensor_scan(
            stk[0:NH, sl], logda[0:NH, sl], zer32[:, 0:128], 0.0, op0=ALU.add, op1=ALU.add),
            waits=([("ss", ss_logdt)] + R.fence("wxq")) if ch == 0 else (),
            reads=("sc0", "zer32"), writes=("stk",))
        R.emit("vector", lambda v, sl=sl: v.scalar_tensor_tensor(
            stk[STKB:STKB + NH, sl], stk[0:NH, sl], -1.0, logdt[0:NH, sl],
            op0=ALU.mult, op1=ALU.add),
            reads=("stk", "sc3"), writes=("stk",))
        R.emit("vector", lambda v, sl=sl, ch=ch: v.tensor_scalar(
            stk[2 * STKB:2 * STKB + NH, sl], stk[STKB:STKB + NH, sl],
            stk[0:NH, ch * 128 + 127:ch * 128 + 128], None, op0=ALU.add),
            reads=("stk",), writes=("stk",),
            inc="vv" if ch == NCH - 1 else None)
    vv_stk = R.val("vv")
    w_cum = R.dma("scalar", lambda s: s.dma_start(out=cum_hbm[:, :], in_=stk[0:NH, :]),
                  waits=[("vv", vv_stk)], stream="wcm", K=1)
    R.emit("scalar", lambda s: s.activation(sc_[2][0:NH, :], stk[0:NH, :], AF.Exp),
           waits=[("vv", vv_stk)], inc="ss")
    ss_ec = R.val("ss")
    w_ec = R.dma("gpsimd", lambda g: g.dma_start(out=ec_hbm[:, :], in_=sc_[2][0:NH, :]),
                 waits=[("ss", ss_ec)], stream="wec", K=1)
    for ch in range(NCH):
        R.emit("tensor", lambda t, ch=ch: t.transpose(
            PB[:, 1024:1024 + 3 * STKB], stk[:, ch * 128:(ch + 1) * 128], eye[0:3 * STKB, 0:3 * STKB]),
            waits=[("vv", vv_stk if ch == 0 else R.val("vv")), ("ss", R.val("ss"))], inc="te")
        te_t = R.val("te")
        R.emit("vector", lambda v, ch=ch: v.tensor_copy(
            TTa[:, ch * NH:(ch + 1) * NH], PB[:, 1024 + STKB:1024 + STKB + NH]),
            waits=[("te", te_t)], inc="vv", reads=("PB",), writes=("TTa",))
        vv_tt = R.val("vv")
        R.emit("scalar", lambda s, ch=ch: s.activation(
            scolT[:, ch * NH:(ch + 1) * NH],
            PB[:, 1024 + 2 * STKB:1024 + 2 * STKB + NH], AF.Exp),
            waits=[("te", te_t)], inc="ss")
        ss_exp = R.val("ss")
        R.emit("tensor", lambda t, ch=ch: t.transpose(
            PB[0:1, 1200:1200 + NH], stk[0:NH, ch * 128 + 127:ch * 128 + 128], eye[0:NH, 0:NH]),
            waits=[("vv", vv_tt), ("ss", ss_exp)], inc="te")
        te_t = R.val("te")
        R.emit("vector", lambda v: v.tensor_copy(cqr[:], PB[0:1, 1200:1200 + NH]),
               waits=[("te", te_t)], inc="vv", reads=("PB",), writes=("cqr",))
        vv_t = R.val("vv")
        R.emit("tensor", lambda t: t.matmul(PB[0:64, 1216:1216 + NH], ones1[:, 0:64], cqr[:],
                                            start=True, stop=True),
               waits=[("vv", vv_t)], inc="te")
        te_t = R.val("te")
        R.emit("scalar", lambda s, ch=ch: s.activation(
            eQa[:, ch * NH:(ch + 1) * NH], PB[0:64, 1216:1216 + NH], AF.Exp),
            waits=[("te", te_t)], inc="ss")
    ss_pre = R.val("ss")

    if cut("pre"):
        return finale()

    NWT = D // 128
    ws_ss_end = {}
    for j in range(NWT):
        b = wf[j % 2]
        wwf = R.dma("gpsimd", lambda g, b=b, j=j: g.dma_start(out=b[:], in_=w_ffnT[j * 128:(j + 1) * 128, :]),
                    waits=[("ss", ws_ss_end.get(j - 2, 0))], stream="wf")
        R.emit("scalar", lambda s, b=b: s.activation(wabs[:], b[:], AF.Abs),
               waits=[wwf, ("vv", R.val("vv"))] + ((F_p0w + [w_ec]) if j == 0 else []), inc="ss")
        ss_t = R.val("ss")
        ws_ss_end[j] = ss_t
        if j == 0:
            R.emit("vector", lambda v: v.reduce_sum(wsacc[:], wabs[:], axis=mybir.AxisListType.X),
                   waits=[("ss", ss_t)], inc="vv", reads=("xnf",), writes=("wsacc",))
        else:
            R.emit("vector", lambda v: v.reduce_sum(lnt[:, 3:4], wabs[:], axis=mybir.AxisListType.X),
                   waits=[("ss", ss_t)], reads=("xnf",), writes=("lnt",))
            R.emit("vector", lambda v: v.tensor_add(wsacc[:], wsacc[:], lnt[:, 3:4]), inc="vv",
                   reads=("wsacc", "lnt"), writes=("wsacc",))
    vv_ws = R.val("vv")
    R.emit("tensor", lambda t: t.transpose(PB[0:1, 0:128], wsacc[:], eye[:, 0:128]),
           waits=[("vv", vv_ws)] + F_cst, inc="te")
    te_ws = R.val("te")
    R.emit("vector", lambda v: v.tensor_copy(wsrow[:], PB[0:1, 0:128]),
           waits=[("te", te_ws)], reads=("PB",), writes=("wsrow",))
    R.emit("vector", lambda v: v.reduce_sum(ws11[:, 0:1], wsrow[:], axis=mybir.AxisListType.X),
           reads=("wsrow",), writes=("ws11",))
    R.emit("vector", lambda v: v.tensor_scalar(ws11[:, 1:2], ws11[:, 0:1], 1.0 / (D * D), None,
                                               op0=ALU.mult), reads=("ws11",), writes=("ws11",))
    R.emit("vector", lambda v: v.tensor_scalar_max(ws11[:, 1:2], ws11[:, 1:2], 1e-6), inc="vv",
           reads=("ws11",), writes=("ws11",))
    vv_wsm = R.val("vv")
    R.emit("scalar", lambda s: s.activation(ws11[:, 3:4], ws11[:, 1:2], AF.Ln),
           waits=[("vv", vv_wsm)])
    R.emit("scalar", lambda s: s.activation(ws11[:, 2:3], ws11[:, 3:4], AF.Exp, scale=cvals[0:1, 2:3]),
           inc="ss")
    ss_wsc = R.val("ss")
    w_wsh = R.dma("scalar", lambda s: s.dma_start(out=ws_hbm[:, 0:2], in_=ws11[:, 1:3]),
                  waits=[("ss", ss_wsc)], stream="wsw", K=1)
    w_wsr = R.dma("gpsimd", lambda g: g.dma_start(out=wscol2[:], in_=ws_hbm[0, 0:2].partition_broadcast(128)),
                  waits=[w_wsh], stream="wsr", K=1)

    wq_vv_end = {}
    wwq_ret = {}
    for j in range(NWT):
        b = wf[j % 2]
        wwf = R.dma("gpsimd", lambda g, b=b, j=j: g.dma_start(out=b[:], in_=w_ffnT[j * 128:(j + 1) * 128, :]),
                    waits=[("vv", wq_vv_end.get(j - 2, 0))] +
                          ([("ss", ws_ss_end[NWT - 1])] if j < 2 else []),
                    stream="wf")
        R.emit("vector", lambda v, b=b: v.tensor_scalar(wqt[:], b[:], wscol2[:, 1:2], MAGIC,
                                                        op0=ALU.mult, op1=ALU.add),
               waits=[wwf, w_wsr, ("te", R.val("te"))] +
                     ((R.fence("wxq") + [w_cum]) if j == 0 else []),
               reads=(f"wf{j % 2}", "wscol2"), writes=("qt",))
        R.emit("vector", lambda v: v.tensor_scalar(wqt[:], wqt[:], MAGIC, -1.0,
                                                   op0=ALU.subtract, op1=ALU.max),
               reads=("qt",), writes=("qt",))
        R.emit("vector", lambda v: v.tensor_scalar(wqb[:], wqt[:], 1.0, wscol2[:, 0:1],
                                                   op0=ALU.min, op1=ALU.mult),
               inc="vv", reads=("qt", "wscol2"), writes=("xqb",),
               waits=[wwq_ret[j - 1]] if j > 0 else [])
        vv_wq = R.val("vv")
        wq_vv_end[j] = vv_wq
        wwq_ret[j] = R.dma("scalar", lambda s, j=j: s.dma_start(
            out=wq_hbm[j * 128:(j + 1) * 128, :], in_=wqb[:]),
            waits=[("vv", vv_wq)], stream="wwq")
    F_wq = R.fence("wwq")

    if cut("w"):
        return finale()

    for bi in range(NCB):
        emit_inproj("xbc", 1 + bi, bi, D + bi * 128, 128, 128)
    F_cv = R.fence("wcv")

    if cut("p1x"):
        return finale()

    R.emit("vector", lambda v: v.memset(s_run[:], 0.0), inc="vv", reads=(), writes=("s_run",))

    xt_te = {}
    xt_vv = {}

    def load_xt(ch):
        xb = XT[ch % 2]
        return R.dma("sync", lambda sy, xb=xb, ch=ch: sy.dma_start_transpose(
            xb[:], xconv_hbm[0:CONV, ch * 128:(ch + 1) * 128]),
            waits=R.fence("wcv") +
                  [("te", xt_te[ch - 2]), ("vv", xt_vv[ch - 2])] if ch - 2 in xt_te
                  else R.fence("wcv"),
            stream="lxt")

    def emit_states(ch, w_xt):
        xb = XT[ch % 2]
        R.emit("vector", lambda v, xb=xb, ch=ch: v.tensor_tensor(
            bsall[:].rearrange("p (h n) -> p h n", h=NH),
            xb[:, D:D + 64].rearrange("p (o n) -> p o n", o=1).to_broadcast([128, NH, 64]),
            scolT[:, ch * NH:(ch + 1) * NH].rearrange("p (h o) -> p h o", o=1).to_broadcast([128, NH, 64]),
            op=ALU.mult),
            waits=[w_xt, ("ss", ss_pre), ("te", R.val("te"))], inc="vv",
            reads=(f"XT{ch % 2}", "scolT"), writes=("bsall",))
        vv_bs = R.val("vv")
        for h in range(NH):
            first = (h % 8 == 0)
            last = (h % 8 == 7) or h == NH - 1
            R.emit("tensor", lambda t, h=h, xb=xb, first=first, last=last:
                   t.matmul(PB[0:64, h * 64:(h + 1) * 64], bsall[:, h * 64:(h + 1) * 64],
                            xb[:, h * 64:(h + 1) * 64], start=first, stop=last),
                   waits=[("vv", vv_bs)] if h == 0 else (), inc="te" if h == NH - 1 else None)
        te_ds = R.val("te")
        R.emit("vector", lambda v, ch=ch: v.tensor_tensor(
            s_run[:].rearrange("p (h n) -> p h n", h=NH),
            s_run[:].rearrange("p (h n) -> p h n", h=NH),
            eQa[:, ch * NH:(ch + 1) * NH].rearrange("p (h o) -> p h o", o=1).to_broadcast([64, NH, 64]),
            op=ALU.mult),
            waits=[("te", te_ds)], reads=("s_run", "eQa"), writes=("s_run",))
        R.emit("vector", lambda v: v.tensor_add(s_run[:], s_run[:], PB[0:64, 0:NH * 64]), inc="vv",
               reads=("s_run", "PB"), writes=("s_run",))

    wxts = {}
    for ch in range(min(2, NCH)):
        wxts[ch] = load_xt(ch)
    for ch in range(NCH):
        emit_states(ch, wxts[ch])
        xt_te[ch] = R.val("te")
        xt_vv[ch] = R.val("vv")
        if ch + 2 < NCH:
            wxts[ch + 2] = load_xt(ch + 2)
    vv_sfin = R.val("vv")

    if cut("p2a"):
        return finale()

    w_cci = R.dma("gpsimd", lambda g: g.dma_start(out=cc_in[:, :], in_=s_run[:]),
                  waits=[("vv", vv_sfin)], stream="cci", K=1)
    R.emit("gpsimd", lambda g: g.collective_compute(
        "AllGather", ALU.bypass, replica_groups=[list(range(n_cores))],
        ins=[cc_in[:]], outs=[cc_out[:]]),
        waits=[w_cci], inc="cc")

    if cut("ag"):
        return finale()

    for bi in range(NZB):
        emit_inproj("z", 1 + NCB + bi, bi, bi * 128, 128, 128)
    F_z = R.fence("wz")

    if cut("p1z"):
        return finale()

    slot_vv = {}
    for r in range(n_cores):
        wsl = R.dma("gpsimd", lambda g, r=r: g.dma_start(
            out=sslot[r % 2][:], in_=cc_out[r * 64:(r + 1) * 64, :]),
            waits=[("cc", 1), ("vv", slot_vv.get(r - 2, 0))] +
                  (([w_cum] + R.fence("wxq")) if r == 0 else []),
            stream="lsl")
        if r == 0:
            R.emit("vector", lambda v, r=r: v.tensor_scalar(
                s_run[:], sslot[r % 2][:], sel[:, r:r + 1], None, op0=ALU.mult),
                waits=[wsl], inc="vv", reads=(f"slots{r % 2}", "sel"), writes=("s_run",))
        else:
            R.emit("vector", lambda v, r=r: v.scalar_tensor_tensor(
                s_run[:], sslot[r % 2][:], sel[:, r:r + 1], s_run[:],
                op0=ALU.mult, op1=ALU.add),
                waits=[wsl], inc="vv",
                reads=(f"slots{r % 2}", "sel", "s_run"), writes=("s_run",))
        slot_vv[r] = R.val("vv")

    if cut("recv"):
        return finale()

    m_te_end = {-1: R.val("te"), -2: R.val("te")}
    m_te_intra = {-1: R.val("te")}
    m_vv_srun = {-1: R.val("vv")}
    m_vv_mall = {-1: R.val("vv")}
    m_vv_clamp = {-1: R.val("vv")}
    m_vv_q1b = {-1: R.val("vv")}
    m_vv_yv = {-1: R.val("vv")}
    m_ss_exp = {-1: R.val("ss")}

    def p2m_chunk(ch, w_xt):
        xb = XT[ch % 2]
        bb = BC2[ch % 2]
        w_bc = R.dma("gpsimd", lambda g, bb=bb, ch=ch: g.dma_start(
            out=bb[:].rearrange("p (gg w) -> p gg w", gg=2),
            in_=xconv_hbm[D:D + 128, ch * 128:(ch + 1) * 128].rearrange("(gg p) w -> p gg w", gg=2)),
            waits=R.fence("wcv") + [("te", m_te_end[ch - 2])], stream="lbc")
        w_cr = []
        for half in range(2):
            w_cr.append(R.dma("gpsimd", lambda g, ch=ch, half=half: g.dma_start(
                out=cumrep[half][:, 0:NHH * 128].rearrange("p (h w) -> p h w", w=128),
                in_=cum_hbm[half * NHH:(half + 1) * NHH, ch * 128:(ch + 1) * 128]
                    .partition_broadcast(128)),
                waits=[w_cum, ("ss", m_ss_exp[ch - 1]), ("vv", m_vv_clamp[ch - 1]),
                       ("te", m_te_end[ch - 2])],
                stream="lcr"))
        w_e = []
        for half in range(2):
            w_e.append(R.dma("gpsimd", lambda g, ch=ch, half=half: g.dma_start(
                out=e_sb[half * 64:(half + 1) * 64, :].rearrange("p (cb t) -> p cb t", t=128),
                in_=ec_hbm.rearrange("(c two) t -> two c t", two=2)[half][:, ch * 128:(ch + 1) * 128]
                    .partition_broadcast(64)),
                waits=[w_ec, ("vv", m_vv_yv[ch - 1])], stream="ler"))
        w_xcm = R.dma("gpsimd", lambda g, ch=ch: g.dma_start(
            out=xcmaj[:].rearrange("p (cc w) -> p cc w", w=128),
            in_=xconv_hbm[0:D, ch * 128:(ch + 1) * 128].rearrange("(cc p) w -> p cc w", p=128)),
            waits=R.fence("wcv") + [("vv", m_vv_yv[ch - 1])], stream="lxc")
        w_sz = R.dma("gpsimd", lambda g, ch=ch: g.dma_start(
            out=szc[:].rearrange("p (cc w) -> p cc w", w=128),
            in_=siluz_hbm[:, ch * 128:(ch + 1) * 128].rearrange("(cc p) w -> p cc w", p=128)),
            waits=F_z + [("vv", m_vv_yv[ch - 1])], stream="lsz")

        R.emit("vector", lambda v: v.tensor_copy(s_bf[:], s_run[:]),
               waits=[("te", m_te_end[ch - 1])], inc="vv",
               reads=("s_run",), writes=("s_bf",))
        vv_sbf = R.val("vv")

        R.emit("tensor", lambda t, bb=bb: t.matmul(PB[:, 0:128], bb[:, 0:128], bb[:, 128:256],
                                                   start=True, stop=True),
               waits=[w_bc, ("vv", m_vv_srun[ch - 1])], inc="te")
        te_g = R.val("te")
        R.emit("vector", lambda v: v.tensor_mul(gmask[:], PB[:, 0:128], caus[:]),
               waits=[("te", te_g)], inc="vv", reads=("PB", "caus"), writes=("gmask",))
        vv_gm = R.val("vv")

        vv_cl = []
        for half in range(2):
            R.emit("vector", lambda v, half=half, ch=ch: v.tensor_tensor(
                cumrep[half][:].rearrange("p (h w) -> p h w", w=128),
                cumrep[half][:].rearrange("p (h w) -> p h w", w=128),
                TTa[:, ch * NH + half * NHH: ch * NH + (half + 1) * NHH]
                    .rearrange("p (h o) -> p h o", o=1).to_broadcast([128, NHH, 128]),
                op=ALU.add),
                waits=[w_cr[half]],
                reads=(f"cumrep{half}", "TTa"), writes=(f"cumrep{half}",))
            R.emit("vector", lambda v, half=half: v.tensor_scalar(
                cumrep[half][:], cumrep[half][:], 40.0, None, op0=ALU.min),
                inc="vv", reads=(f"cumrep{half}",), writes=(f"cumrep{half}",))
            vv_cl.append(R.val("vv"))
        m_vv_clamp[ch] = vv_cl[1]
        for half in range(2):
            R.emit("scalar", lambda s, half=half: s.activation(
                dall[:, half * NHH * 128:(half + 1) * NHH * 128], cumrep[half][:], AF.Exp),
                waits=[("vv", vv_cl[half]), ("vv", m_vv_mall[ch - 1])] +
                      (F_z if ch == 0 and half == 0 else []),
                inc="ss" if half == 1 else None)
        ss_d = R.val("ss")
        m_ss_exp[ch] = ss_d

        R.emit("vector", lambda v: v.tensor_tensor(
            mall[:].rearrange("p (h w) -> p h w", w=128),
            dall[:].rearrange("p (h w) -> p h w", w=128),
            gmask[:].rearrange("p (o w) -> p o w", o=1).to_broadcast([128, NH, 128]),
            op=ALU.mult),
            waits=[("ss", ss_d), ("te", m_te_intra[ch - 1])], inc="vv",
            reads=("dall", "gmask"), writes=("mall",))
        vv_m = R.val("vv")
        m_vv_mall[ch] = vv_m

        for h in range(NH):
            cb, s_ = divmod(h, 2)
            R.emit("tensor", lambda t, h=h, cb=cb, s_=s_, xb=xb:
                   t.matmul(PA[s_ * 64:(s_ + 1) * 64, cb * 128:(cb + 1) * 128],
                            xb[:, h * 64:(h + 1) * 64], mall[:, h * 128:(h + 1) * 128],
                            start=True, stop=True),
                   waits=[("vv", vv_m), w_xt, ("vv", m_vv_q1b[ch - 1])] if h == 0 else (),
                   inc="te" if h == NH - 1 else None)
        te_intra = R.val("te")
        m_te_intra[ch] = te_intra

        for pr in range(NPAIR):
            R.emit("tensor", lambda t, pr=pr, bb=bb:
                   t.matmul(PB[:, pr * 128:(pr + 1) * 128],
                            s_bf[:, pr * 128:(pr + 1) * 128], bb[:, 128:256],
                            start=True, stop=True),
                   waits=[("vv", vv_sbf), ("vv", vv_gm), w_bc] if pr == 0 else (),
                   inc="te" if pr == NPAIR - 1 else None)
        te_inter = R.val("te")

        R.emit("vector", lambda v: v.tensor_mul(q1[:], PB[:, 0:D], e_sb[:]),
               waits=[("te", te_inter)] + w_e, inc="vv",
               reads=("PB", "e_sb"), writes=("q1",))
        vv_q1 = R.val("vv")
        R.emit("vector", lambda v: v.scalar_tensor_tensor(q1[:], PA[:, 0:D], 1.0, q1[:],
                                                          op0=ALU.mult, op1=ALU.add),
               waits=[("te", te_intra)], inc="vv", reads=("PA", "q1"), writes=("q1",))
        m_vv_q1b[ch] = R.val("vv")
        R.emit("vector", lambda v: v.tensor_tensor(
            e_sb[:].rearrange("p (cb t) -> p cb t", t=128),
            xcmaj[:].rearrange("p (cb t) -> p cb t", t=128),
            dmaj[:].rearrange("p (cb o) -> p cb o", o=1).to_broadcast([128, NCT, 128]),
            op=ALU.mult),
            waits=[w_xcm], reads=("xcmaj", "dmaj"), writes=("e_sb",))
        R.emit("vector", lambda v: v.tensor_add(q1[:], q1[:], e_sb[:]),
               reads=("q1", "e_sb"), writes=("q1",))
        R.emit("vector", lambda v, ch=ch: v.tensor_tensor(
            ych[:].rearrange("p (cb t) -> p cb t", t=T)[:, :, ch * 128:(ch + 1) * 128],
            q1[:].rearrange("p (cb t) -> p cb t", t=128),
            szc[:].rearrange("p (cb t) -> p cb t", t=128),
            op=ALU.mult),
            waits=[w_sz], inc="vv", reads=("q1", "szc"), writes=(f"ychc{ch}",))
        m_vv_yv[ch] = R.val("vv")

        R.emit("vector", lambda v, xb=xb, ch=ch: v.tensor_tensor(
            bsall[:].rearrange("p (h n) -> p h n", h=NH),
            xb[:, D:D + 64].rearrange("p (o n) -> p o n", o=1).to_broadcast([128, NH, 64]),
            scolT[:, ch * NH:(ch + 1) * NH].rearrange("p (h o) -> p h o", o=1).to_broadcast([128, NH, 64]),
            op=ALU.mult), waits=[w_xt], inc="vv",
            reads=(f"XT{ch % 2}", "scolT"), writes=("bsall",))
        vv_bs = R.val("vv")
        for h in range(NH):
            first = (h % 8 == 0)
            last = (h % 8 == 7) or h == NH - 1
            R.emit("tensor", lambda t, h=h, xb=xb, first=first, last=last:
                   t.matmul(PB[0:64, h * 64:(h + 1) * 64], bsall[:, h * 64:(h + 1) * 64],
                            xb[:, h * 64:(h + 1) * 64], start=first, stop=last),
                   waits=[("vv", vv_bs), ("vv", vv_q1)] if h == 0 else (),
                   inc="te" if h == NH - 1 else None)
        te_ds = R.val("te")

        R.emit("vector", lambda v, ch=ch: v.tensor_tensor(
            s_run[:].rearrange("p (h n) -> p h n", h=NH),
            s_run[:].rearrange("p (h n) -> p h n", h=NH),
            eQa[:, ch * NH:(ch + 1) * NH].rearrange("p (h o) -> p h o", o=1).to_broadcast([64, NH, 64]),
            op=ALU.mult),
            reads=("s_run", "eQa"), writes=("s_run",))
        R.emit("vector", lambda v: v.tensor_add(s_run[:], s_run[:], PB[0:64, 0:NH * 64]),
               waits=[("te", te_ds)], inc="vv",
               reads=("s_run", "PB"), writes=("s_run",))
        m_vv_srun[ch] = R.val("vv")
        m_te_end[ch] = R.val("te")

    xt_te.clear()
    xt_vv.clear()
    xt_te[-2] = xt_te[-1] = R.val("te")
    xt_vv[-2] = xt_vv[-1] = R.val("vv")
    wxts = {}
    for ch in range(min(2, NCH)):
        wxts[ch] = load_xt(ch)
    for ch in range(NCH):
        p2m_chunk(ch, wxts[ch])
        xt_te[ch] = R.val("te")
        xt_vv[ch] = R.val("vv")
        if ch + 2 < NCH:
            wxts[ch + 2] = load_xt(ch + 2)
    vv_p2 = R.val("vv")

    if cut("p2m"):
        return finale()

    for cb in range(NCT):
        R.dma("sync", lambda sy, cb=cb: sy.dma_start_transpose(
            xq_ch[:, cb * T:(cb + 1) * T], xq_hbm[0:T, cb * 128:(cb + 1) * 128]),
            waits=R.fence("wxq") + [("vv", vv_p2), ("te", R.val("te"))], stream="lxq")
    F_xq = R.fence("lxq")

    p3_vv_end = {}
    p3_wout = {}
    p3_te_end = {}
    te_p2 = R.val("te")
    for osi in range(c["NOS"]):
        wo = wos[osi % 2]
        wq = wqs[osi % 2]
        w_wo = R.dma("gpsimd", lambda g, wo=wo, osi=osi: g.dma_start(
            out=wo[:, 0:NCT * OSW].rearrange("p (cc w) -> p cc w", w=OSW),
            in_=w_outT[:, osi * OSW:(osi + 1) * OSW].rearrange("(cc p) w -> p cc w", p=128)),
            waits=[("te", p3_te_end.get(osi - 2, te_p2)), ("vv", vv_p2)], stream="lwo")
        w_wq = R.dma("gpsimd", lambda g, wq=wq, osi=osi: g.dma_start(
            out=wq[:, 0:NCT * OSW].rearrange("p (cc w) -> p cc w", w=OSW),
            in_=wq_hbm[:, osi * OSW:(osi + 1) * OSW].rearrange("(cc p) w -> p cc w", p=128)),
            waits=F_wq, stream="lwo")
        for ti in range(NT):
            it = osi * NT + ti
            ubuf = uts[it % 4]
            obuf = outs[it % 4]
            w_u = R.dma("gpsimd", lambda g, ubuf=ubuf, ti=ti, osi=osi: g.dma_start(
                out=ubuf[:], in_=u_loc[ti * 128:(ti + 1) * 128, osi * OSW:(osi + 1) * OSW]),
                waits=[("vv", p3_vv_end.get(it - 4, vv_p2))], stream="lu3", K=4)
            pv = PA[:, (it % 4) * 512:(it % 4) * 512 + OSW]
            for ci in range(NCT):
                wl = ([w_wo, w_wq, ("vv", p3_vv_end.get(it - 4, vv_p2))] + F_xq) \
                    if ci == 0 else []
                R.emit("tensor", lambda t, pv=pv, ci=ci, ti=ti, wo=wo:
                       t.matmul(pv, ych[:, ci * T + ti * 128:ci * T + (ti + 1) * 128],
                                wo[:, ci * OSW:(ci + 1) * OSW], start=(ci == 0), stop=False),
                       waits=wl)
            for ci in range(NCT):
                R.emit("tensor", lambda t, pv=pv, ci=ci, ti=ti, wq=wq:
                       t.matmul(pv, xq_ch[:, ci * T + ti * 128:ci * T + (ti + 1) * 128],
                                wq[:, ci * OSW:(ci + 1) * OSW], start=False, stop=(ci == NCT - 1)),
                       inc="te" if ci == NCT - 1 else None)
            te_t = R.val("te")
            R.emit("vector", lambda v, pv=pv, ubuf=ubuf, obuf=obuf: v.scalar_tensor_tensor(
                obuf[:], pv, 1.0, ubuf[:], op0=ALU.mult, op1=ALU.add),
                waits=[("te", te_t), w_u] +
                      ([p3_wout[it - 4]] if it - 4 in p3_wout else []), inc="vv",
                reads=("PA", f"uts{it % 4}"), writes=(f"outs{it % 4}",))
            p3_vv_end[it] = R.val("vv")
            p3_wout[it] = R.dma("scalar", lambda s, obuf=obuf, ti=ti, osi=osi: s.dma_start(
                out=out_loc[ti * 128:(ti + 1) * 128, osi * OSW:(osi + 1) * OSW], in_=obuf[:]),
                waits=[("vv", p3_vv_end[it])], stream="wout", K=4)
        p3_te_end[osi] = R.val("te")

    return finale()


def make_core_inputs(c, u, conv_w, conv_b, A_log, D_param, dt_bias):
    D, T, NH, NCB = c["D"], c["T"], c["NH"], c["NCB"]
    n_cores = c["n_cores"]
    cw = np.ascontiguousarray(conv_w.reshape(NCB, 128, 4).transpose(1, 0, 2).reshape(128, NCB * 4))
    cb = np.ascontiguousarray(conv_b.reshape(NCB, 128).T)
    dtb = np.ascontiguousarray(dt_bias[:, None])
    alg = np.ascontiguousarray(A_log[:, None])
    NCT = D // 128
    dmaj = np.empty((128, NCT), np.float32)
    for cbi in range(NCT):
        for p in range(128):
            dmaj[p, cbi] = D_param[(cbi * 128 + p) // 64]
    caus = np.triu(np.ones((128, 128), np.float32))
    eye = np.eye(128, dtype=np.float32)
    ones1 = np.ones((1, 128), np.float32)
    onecol = np.ones((128, 1), np.float32)
    zer32 = np.zeros((NH, 128), np.float32)
    zpad = np.zeros((16, D), BF)
    cvals = np.zeros((128, 8), np.float32)
    cvals[:, 0] = 1e-5
    cvals[:, 1] = -0.5
    cvals[:, 2] = -1.0
    cvals[:, 3] = 1.0
    maps = []
    for k in range(n_cores):
        b, half = k // 2, k % 2
        u_b = u[b]
        u_slice = np.ascontiguousarray(u_b[half * T:(half + 1) * T])
        halo = np.zeros((128, D), np.float32)
        if half == 1:
            halo[0:3] = u_b[T - 3:T]
        selv = np.zeros((64, n_cores), np.float32)
        if half == 1:
            selv[:, k - 1] = 1.0
        maps.append(dict(
            u_loc=u_slice, u_halo=halo,
            conv_w_l=cw, conv_b_l=cb, dt_bias_c=dtb, a_log_c=alg, dmaj_in=dmaj,
            sel_in=selv, caus_in=caus, eye_in=eye, ones_in=ones1,
            onecol_in=onecol, zer32_in=zer32, zpad_in=zpad, cvals_in=cvals,
        ))
    return maps


_NC_CACHE = {}


def kernel(L0_attn_u, L0_attn_W_in, L0_attn_conv_w, L0_attn_conv_b, L0_attn_W_out,
           L0_attn_A_log, L0_attn_D_param, L0_attn_dt_bias, L0_ffn_weight):
    c = mkcfg(D=2048, T=2048, n_cores=8)
    u = np.ascontiguousarray(np.asarray(L0_attn_u, dtype=np.float32))
    maps = make_core_inputs(
        c, u,
        np.ascontiguousarray(np.asarray(L0_attn_conv_w, np.float32).reshape(c["CONV"], 4)),
        np.ascontiguousarray(np.asarray(L0_attn_conv_b, np.float32)),
        np.ascontiguousarray(np.asarray(L0_attn_A_log, np.float32)),
        np.ascontiguousarray(np.asarray(L0_attn_D_param, np.float32)),
        np.ascontiguousarray(np.asarray(L0_attn_dt_bias, np.float32)),
    )
    w_inT = np.ascontiguousarray(np.asarray(L0_attn_W_in, np.float32).T)
    w_outT = np.ascontiguousarray(np.asarray(L0_attn_W_out, np.float32).T)
    w_ffnT = np.ascontiguousarray(np.asarray(L0_ffn_weight, np.float32).T)
    for m in maps:
        m["w_inT"] = w_inT
        m["w_outT"] = w_outT
        m["w_ffnT"] = w_ffnT
    key = (c["D"], c["T"], c["n_cores"])
    if key not in _NC_CACHE:
        _NC_CACHE[key] = build_nc(c)
    nc = _NC_CACHE[key]
    res = run_bass_kernel_spmd(nc, maps, core_ids=list(range(c["n_cores"])))
    B, L = u.shape[0], u.shape[1]
    out = np.empty((B, L, c["D"]), np.float32)
    for k in range(c["n_cores"]):
        b, half = k // 2, k % 2
        out[b, half * c["T"]:(half + 1) * c["T"], :] = res.results[k]["out_loc"]
    return out
